# version 21
# baseline (speedup 1.0000x reference)
import numpy as np
import concourse.bass as bass
import concourse.mybir as mybir

F32 = mybir.dt.float32
F32R = mybir.dt.float32r
BF16 = mybir.dt.bfloat16
ADD = mybir.AluOpType.add
SUB = mybir.AluOpType.subtract
MUL = mybir.AluOpType.mult
ISEQ = mybir.AluOpType.is_equal
MAXOP = mybir.AluOpType.max
Act = mybir.ActivationFunctionType
AX = mybir.AxisListType.X

ENGINES = ("sync", "vector", "scalar", "tensor", "gpsimd")


class Sem:
    def __init__(self, handle):
        self.h = handle
        self.n = 0


def build(cfg):
    D = cfg["D"]; TF = cfg["TF"]; TO = cfg["TO"]; H = cfg["H"]
    HD = 64; E = 8; HID = 256; TT = 512
    ND = D // 128
    NH2 = HID // 128
    NTF = TF // TT
    NTO = TO // TT
    NKT = TF // 128
    NQG = TO // TT
    DH = H * HD
    NDH = DH // 128
    NO3 = 3 * NDH
    SC = min(1024, TF)
    NSC = TF // SC
    NTSC = SC // TT
    NC4 = TT // 128
    LN_EPS = 1e-5
    assert D == 1024 and TO * 2 == TF and TT == 512

    nc = bass.Bass("TRN2", target_bir_lowering=False)

    x_d = nc.dram_tensor("x_t", [D, TF], F32, kind="ExternalInput").ap()
    wqkv_d = nc.dram_tensor("w_qkv", [D, 3 * DH], F32, kind="ExternalInput").ap()
    wproj_d = nc.dram_tensor("w_proj", [DH, D], F32, kind="ExternalInput").ap()
    bproj_d = nc.dram_tensor("b_proj", [128, ND], F32, kind="ExternalInput").ap()
    wgT_d = nc.dram_tensor("wg_t", [D, E], F32, kind="ExternalInput").ap()
    lnsT_d = nc.dram_tensor("ln_scale_t", [D, E], F32, kind="ExternalInput").ap()
    lnbT_d = nc.dram_tensor("ln_bias_t", [D, E], F32, kind="ExternalInput").ap()
    w1_d = nc.dram_tensor("w1", [E, D, HID], F32, kind="ExternalInput").ap()
    b1_d = nc.dram_tensor("b1", [E, HID], F32, kind="ExternalInput").ap()
    w2_d = nc.dram_tensor("w2", [E, HID, D], F32, kind="ExternalInput").ap()
    b2_d = nc.dram_tensor("b2", [E, D], F32, kind="ExternalInput").ap()
    eye_d = nc.dram_tensor("eye_d", [128, 128], F32, kind="ExternalInput").ap()
    msel_d = nc.dram_tensor("msel_d", [128, 128], F32, kind="ExternalInput").ap()
    sel8_d = nc.dram_tensor("sel8_d", [E, E * 128], F32, kind="ExternalInput").ap()
    out_d = nc.dram_tensor("out_t", [D, TO], F32, kind="ExternalOutput").ap()

    x1sp = nc.dram_tensor("x1sp", [D, TF], F32, kind=("ExternalOutput" if cfg.get("dbg") else "Internal")).ap()
    ksp = nc.dram_tensor("ksp", [H, 128, TF], BF16, kind="Internal").ap()
    vsp = nc.dram_tensor("vsp", [H, TF, 128], F32R, kind="Internal").ap()
    qsp = nc.dram_tensor("qsp", [H, 128, TO], BF16, kind="Internal").ap()
    aosp = nc.dram_tensor("aosp", [2, DH, TO], BF16, kind=("ExternalOutput" if cfg.get("dbg") else "Internal")).ap()
    wqkvs = nc.dram_tensor("wqkvs", [2, D, 3 * DH], BF16, kind="Internal").ap()
    wprojs = nc.dram_tensor("wprojs", [2, DH, D], BF16, kind="Internal").ap()
    w1s = nc.dram_tensor("w1s", [2, E, D, HID], BF16, kind="Internal").ap()
    w2s = nc.dram_tensor("w2s", [2, E, HID, D], BF16, kind="Internal").ap()

    from contextlib import ExitStack
    es = ExitStack()

    def sbt(name, shape, dt=F32):
        return es.enter_context(nc.sbuf_tensor(name, shape, dt))

    eye = sbt("eye", [128, 128])
    msel = sbt("msel", [128, 128], F32R)
    msel_f = sbt("msel_f", [128, 128])
    onesq = sbt("onesq", [65, 64])
    rec65 = sbt("rec65", [65, 512])
    sel8 = sbt("sel8", [E, E * 128])
    ones128 = sbt("ones128", [128, 1])
    ones1r = sbt("ones1r", [1, 128])
    onesb = sbt("onesb", [128, 1], BF16)
    bproj = sbt("bproj", [128, ND])
    wgT = sbt("wgT", [128, ND * E])
    lnsT = sbt("lnsT", [128, ND * E])
    lnbT = sbt("lnbT", [128, ND * E])
    b1eff = sbt("b1eff", [128, E * NH2])
    b2eff = sbt("b2eff", [E, D])
    st_s1 = sbt("st_s1", [1, TT]); st_s2 = sbt("st_s2", [1, TT])
    st_mu = sbt("st_mu", [1, TT]); st_v = sbt("st_v", [1, TT])
    st_t0 = sbt("st_t0", [1, TT]); st_t1 = sbt("st_t1", [1, TT])
    st_mc = sbt("st_mc", [1, TT]); st_rc = sbt("st_rc", [1, TT])
    rmax = sbt("rmax", [128, NC4]); found = sbt("found", [128, NC4])
    eqb = sbt("eqb", [128, NC4]); equ = sbt("equ", [128, NC4])
    eqall = sbt("eqall", [128, E * NC4])
    mtok = sbt("mtok", [128, NC4 * E])
    masks8 = sbt("masks8", [E, TT])
    qn_tab = sbt("qn_tab", [2, 8]); kn_tab = sbt("kn_tab", [2, 8])
    nrm2 = sbt("nrm2", [2, 2]); hsel = sbt("hsel", [128, 2])
    lg8 = sbt("lg8", [E, TT])
    eps1 = sbt("eps1", [1, 1])
    vqr = sbt("vqr", [128, TF], F32R)
    p_sr = sbt("p_sr", [128, 4 * TT], F32R)
    avsbr = sbt("avsbr", [128, TT], F32R)
    wgh = sbt("wgh", [128, ND * E], BF16)
    wgl = sbt("wgl", [128, ND * E], BF16)

    ARW = cfg.get("ARW", 65536)
    AR = sbt("AR", [128, ARW], BF16)

    class Arena:
        def __init__(self):
            self.off = 0

        def reset(self):
            self.off = 0

        def bf16(self, parts, w):
            o = (self.off + 3) & ~3
            self.off = o + w
            assert self.off <= ARW, f"arena overflow {self.off} > {ARW}"
            return AR[0:parts, o:o + w]

        def f32(self, parts, w):
            return self.bf16(parts, 2 * w).bitcast(F32)

    A = Arena()

    PSALL = es.enter_context(nc.psum_tensor("psall", [128, 4096], F32))
    PS = [PSALL[0:128, i * 512:(i + 1) * 512] for i in range(8)]

    _DS = {}

    def DS(key):
        if key not in _DS:
            _DS[key] = Sem(es.enter_context(nc.semaphore(f"d_{key}")))
        return _DS[key]
    sTE = Sem(es.enter_context(nc.semaphore("sTE")))
    sVE = Sem(es.enter_context(nc.semaphore("sVE")))
    sSC = Sem(es.enter_context(nc.semaphore("sSC")))
    sPH = Sem(es.enter_context(nc.semaphore("sPH")))

    streams = {e: [] for e in ENGINES}
    cons = {}
    STOP = cfg.get("stop", 9)

    class _StopEmission(Exception):
        pass

    stopped = [False]

    def maybe_stop(k):
        if STOP <= k:
            stopped[0] = True

    def op(eng, name, *args, inc=None, k=1, **kw):
        if stopped[0]:
            return
        if inc is not None:
            inc.n += k
            streams[eng].append(lambda e, n=name, a=args, kk=kw, h=inc.h, K=k:
                                getattr(e, n)(*a, **kk).then_inc(h, K))
        else:
            streams[eng].append(lambda e, n=name, a=args, kk=kw: getattr(e, n)(*a, **kk))

    def wait(eng, sem, val=None):
        if stopped[0]:
            return
        v = sem.n if val is None else val
        if v > 0:
            streams[eng].append(lambda e, h=sem.h, vv=v: e.wait_ge(h, vv))

    def mile(eng, sem, k=1):
        if stopped[0]:
            return
        sem.n += k
        streams[eng].append(lambda e, h=sem.h, kk=k: e.maybe_drain_then_inc((h, kk)))

    def drain(eng):
        if stopped[0]:
            return
        streams[eng].append(lambda e: e.drain())

    def dma(eng, out_ap, in_ap, key):
        if stopped[0]:
            return
        sem = DS(key)
        sem.n += 16
        streams[eng].append(lambda e, o=out_ap, i=in_ap, h=sem.h:
                            e.dma_start(out=o, in_=i).then_inc(h, 16))

    def wc(eng, key):
        if key in cons:
            s, v = cons[key]
            wait(eng, s, v)

    def sc_(key, sem):
        cons[key] = (sem, sem.n)

    def barrier():
        for e in ENGINES:
            mile(e, sPH, 1)
        tgt = sPH.n
        for e in ENGINES:
            wait(e, sPH, tgt)

    if cfg.get("skip_p0"):
        stopped[0] = True if False else stopped[0]
    _P0ON = not cfg.get("skip_p0")
    stopped[0] = not _P0ON
    dma("sync", eye[:], eye_d[:], "res")
    dma("sync", msel_f[:], msel_d[:], "res")
    dma("sync", sel8[:], sel8_d[:], "res")
    dma("sync", bproj[:], bproj_d[:], "res")
    for dt in range(ND):
        dma("sync", wgT[:, dt * E:(dt + 1) * E], wgT_d[dt * 128:(dt + 1) * 128, :], "res")
        dma("sync", lnsT[:, dt * E:(dt + 1) * E], lnsT_d[dt * 128:(dt + 1) * 128, :], "res")
        dma("sync", lnbT[:, dt * E:(dt + 1) * E], lnbT_d[dt * 128:(dt + 1) * 128, :], "res")
    dma("sync", b2eff[:], b2_d[:], "res")
    wait("vector", DS("res"))
    op("vector", "memset", ones128[:], 1.0)
    op("vector", "memset", ones1r[:], 1.0)
    op("vector", "memset", onesb[:], 1.0)
    op("vector", "memset", eps1[:], 1e-5)
    op("vector", "memset", onesq[:], 1.0)
    op("vector", "memset", qn_tab[:], 0.0)
    op("vector", "memset", kn_tab[:], 0.0)
    op("vector", "memset", hsel[:], 0.0)
    drain("vector")
    op("vector", "memset", hsel[0:64, 0:1], 1.0)
    op("vector", "memset", hsel[64:128, 1:2], 1.0)
    op("vector", "tensor_scalar", b2eff[:], b2eff[:], 0.5, None, op0=MUL)
    op("vector", "tensor_copy", msel[:], msel_f[:])
    op("vector", "tensor_copy", wgh[:], wgT[:])
    drain("vector")
    op("vector", "tensor_tensor", wgl[:], wgT[:], wgh[:], op=SUB)
    mile("vector", sVE)

    A.reset()
    wt = [A.f32(128, 1024) for _ in range(2)]
    whi = [A.bf16(128, 1024) for _ in range(2)]
    wlo = [A.bf16(128, 1024) for _ in range(2)]
    w1f = [A.f32(128, HID) for _ in range(2)]
    b1t = A.f32(128, E * NH2)
    p0c = [0]

    def p0j():
        j = p0c[0] % 2
        p0c[0] += 1
        return j

    def split_cols(j, src_ap, cols):
        op("vector", "tensor_copy", whi[j][:, 0:cols], src_ap)
        drain("vector")
        op("vector", "tensor_tensor", wlo[j][:, 0:cols], src_ap, whi[j][:, 0:cols], op=SUB)
        mile("vector", sVE)

    P0S = cfg.get("p0_sec", 31)
    CW = 1024
    for dt in range(ND if P0S & 1 else 0):
        for c0 in range(0, 3 * DH, CW):
            cols = min(CW, 3 * DH - c0)
            j = p0j()
            wc("sync", f"wt{j}")
            dma("sync", wt[j][:, 0:cols], wqkv_d[dt * 128:(dt + 1) * 128, c0:c0 + cols], f"wt{j}")
            wait("vector", DS(f"wt{j}"))
            wc("vector", f"whilo{j}")
            qc = min(max(DH - c0, 0), cols)
            if qc > 0:
                op("vector", "tensor_scalar", wt[j][:, 0:qc], wt[j][:, 0:qc], 0.125, None, op0=MUL)
                drain("vector")
            split_cols(j, wt[j][:, 0:cols], cols)
            sc_(f"wt{j}", sVE)
            wait("gpsimd", sVE)
            dma("gpsimd", wqkvs[0, dt * 128:(dt + 1) * 128, c0:c0 + cols], whi[j][:, 0:cols], f"st_w{j}")
            dma("gpsimd", wqkvs[1, dt * 128:(dt + 1) * 128, c0:c0 + cols], wlo[j][:, 0:cols], f"st_w{j}")
            sc_(f"whilo{j}", DS(f"st_w{j}"))
    for dt in range(NDH if P0S & 2 else 0):
        j = p0j()
        wc("sync", f"wt{j}")
        dma("sync", wt[j][:, 0:D], wproj_d[dt * 128:(dt + 1) * 128, :], f"wt{j}")
        wait("vector", DS(f"wt{j}"))
        wc("vector", f"whilo{j}")
        split_cols(j, wt[j][:, 0:D], D)
        sc_(f"wt{j}", sVE)
        wait("gpsimd", sVE)
        dma("gpsimd", wprojs[0, dt * 128:(dt + 1) * 128, :], whi[j][:, 0:D], f"st_w{j}")
        dma("gpsimd", wprojs[1, dt * 128:(dt + 1) * 128, :], wlo[j][:, 0:D], f"st_w{j}")
        sc_(f"whilo{j}", DS(f"st_w{j}"))
    for e_i in range(E if P0S & 4 else 0):
        for dt in range(ND):
            j = p0j()
            wc("sync", f"wt{j}")
            dma("sync", wt[j][:, 0:HID], w1_d[e_i, dt * 128:(dt + 1) * 128, :], f"wt{j}")
            wait("vector", DS(f"wt{j}"))
            wc("vector", f"whilo{j}")
            wc("vector", f"w1f{j}")
            op("vector", "tensor_scalar", w1f[j][:], wt[j][:, 0:HID],
               lnsT[:, dt * E + e_i:dt * E + e_i + 1], None, op0=MUL)
            drain("vector")
            split_cols(j, w1f[j][:], HID)
            sc_(f"wt{j}", sVE)
            wait("gpsimd", sVE)
            dma("gpsimd", w1s[0, e_i, dt * 128:(dt + 1) * 128, :], whi[j][:, 0:HID], f"st_w{j}")
            dma("gpsimd", w1s[1, e_i, dt * 128:(dt + 1) * 128, :], wlo[j][:, 0:HID], f"st_w{j}")
            sc_(f"whilo{j}", DS(f"st_w{j}"))
            wait("tensor", sVE)
            for hh in range(NH2):
                op("tensor", "matmul",
                   PS[0][0:128, (e_i * NH2 + hh) * ND + dt:(e_i * NH2 + hh) * ND + dt + 1],
                   w1f[j][:, hh * 128:(hh + 1) * 128],
                   lnbT[:, dt * E + e_i:dt * E + e_i + 1],
                   start=True, stop=True, inc=sTE)
            sc_(f"w1f{j}", sTE)
    for e_i in range(E if P0S & 4 else 0):
        for hh in range(NH2):
            dma("sync", b1t[:, e_i * NH2 + hh:e_i * NH2 + hh + 1],
                b1_d[e_i:e_i + 1, hh * 128:(hh + 1) * 128], "b1t")
    if P0S & 4:
        wait("vector", DS("b1t"))
        wait("vector", sTE)
        op("vector", "tensor_reduce", b1eff[:],
           PS[0][0:128, 0:E * NH2 * ND].rearrange("p (c d) -> p c d", d=ND),
           op=ADD, axis=AX)
        drain("vector")
        op("vector", "tensor_tensor", b1eff[:], b1eff[:], b1t[:], op=ADD)
        mile("vector", sVE)
    for e_i in range(E if P0S & 8 else 0):
        for hh in range(NH2):
            j = p0j()
            wc("sync", f"wt{j}")
            dma("sync", wt[j][:, 0:D], w2_d[e_i, hh * 128:(hh + 1) * 128, :], f"wt{j}")
            wait("vector", DS(f"wt{j}"))
            wc("vector", f"whilo{j}")
            op("vector", "tensor_scalar", wt[j][:, 0:D], wt[j][:, 0:D], 0.5, None, op0=MUL)
            drain("vector")
            split_cols(j, wt[j][:, 0:D], D)
            sc_(f"wt{j}", sVE)
            wait("gpsimd", sVE)
            dma("gpsimd", w2s[0, e_i, hh * 128:(hh + 1) * 128, :], whi[j][:, 0:D], f"st_w{j}")
            dma("gpsimd", w2s[1, e_i, hh * 128:(hh + 1) * 128, :], wlo[j][:, 0:D], f"st_w{j}")
            sc_(f"whilo{j}", DS(f"st_w{j}"))
    stopped[0] = False
    [wait("gpsimd", DS(f"st_w{jj}")) for jj in range(2) if _P0ON and DS(f"st_w{jj}").n > 0]
    barrier()
    cons.clear()
    maybe_stop(0)

    x_view = x_d.rearrange("(a p) t -> p a t", p=128)
    x1_view = x1sp.rearrange("(a p) t -> p a t", p=128)

    def ln_stats_fused(xap, iA, iB, sq, double):
        wc("tensor", f"bk{iA}")
        for dt in range(ND):
            op("tensor", "matmul", PS[iA][0:1, 0:TT], ones128[:],
               xap[:, dt * TT:(dt + 1) * TT],
               start=(dt == 0), stop=(dt == ND - 1), inc=sTE)
        wc("tensor", f"bk{iB}")
        for h4 in range(2):
            wc("scalar", "sq")
            for d4 in range(4):
                dt = h4 * 4 + d4
                op("scalar", "activation", sq[:, d4 * TT:(d4 + 1) * TT],
                   xap[:, dt * TT:(dt + 1) * TT], Act.Square, inc=sSC)
            wait("tensor", sSC)
            for d4 in range(4):
                dt = h4 * 4 + d4
                op("tensor", "matmul", PS[iB][0:1, 0:TT], ones128[:],
                   sq[:, d4 * TT:(d4 + 1) * TT],
                   start=(dt == 0), stop=(dt == ND - 1), inc=sTE)
            sc_("sq", sTE)
        wait("vector", sTE)
        op("vector", "tensor_scalar", st_mu[:], PS[iA][0:1, 0:TT], 1.0 / D, None, op0=MUL)
        op("vector", "tensor_scalar", st_s2[:], PS[iB][0:1, 0:TT], 1.0 / D, None, op0=MUL)
        drain("vector")
        op("vector", "tensor_tensor", st_t0[:], st_mu[:], st_mu[:], op=MUL)
        drain("vector")
        op("vector", "tensor_tensor", st_v[:], st_s2[:], st_t0[:], op=SUB)
        mile("vector", sVE)
        sc_(f"bk{iA}", sVE); sc_(f"bk{iB}", sVE)
        wait("scalar", sVE)
        op("scalar", "activation", st_t1[:], st_v[:], Act.Sqrt, bias=eps1[0:1, 0:1], inc=sSC)
        wait("vector", sSC)
        op("vector", "reciprocal", st_t0[:], st_t1[:])
        drain("vector")
        if not double:
            op("vector", "tensor_copy", st_rc[:], st_t0[:])
            op("vector", "tensor_copy", st_mc[:], st_mu[:])
            mile("vector", sVE)
            return
        op("vector", "tensor_tensor", st_s2[:], st_t0[:], st_t0[:], op=MUL)
        drain("vector")
        op("vector", "tensor_tensor", st_s1[:], st_v[:], st_s2[:], op=MUL)
        mile("vector", sVE)
        wait("scalar", sVE)
        op("scalar", "activation", st_t1[:], st_s1[:], Act.Sqrt, bias=eps1[0:1, 0:1], inc=sSC)
        wait("vector", sSC)
        op("vector", "reciprocal", st_s2[:], st_t1[:])
        drain("vector")
        op("vector", "tensor_tensor", st_rc[:], st_t0[:], st_s2[:], op=MUL)
        op("vector", "tensor_copy", st_mc[:], st_mu[:])
        mile("vector", sVE)

    def replicate_rows(iA, iB, rep0, rep1):
        wait("tensor", sVE)
        wc("tensor", f"bk{iA}"); wc("tensor", f"bk{iB}")
        op("tensor", "matmul", PS[iA][0:128, 0:TT], ones1r[:], st_mc[:], start=True, stop=True, inc=sTE)
        op("tensor", "matmul", PS[iB][0:128, 0:TT], ones1r[:], st_rc[:], start=True, stop=True, inc=sTE)
        wait("vector", sTE)
        op("vector", "tensor_copy", rep0[:], PS[iA][0:128, 0:TT])
        op("vector", "tensor_copy", rep1[:], PS[iB][0:128, 0:TT])
        mile("vector", sVE)
        sc_(f"bk{iA}", sVE); sc_(f"bk{iB}", sVE)

    def affine_split_nd(xap, rep0, rep1, xw, outhi, outlo):
        for dt in range(ND):
            op("vector", "tensor_tensor", xw[:, dt * TT:(dt + 1) * TT],
               xap[:, dt * TT:(dt + 1) * TT], rep0[:], op=SUB)
        drain("vector")
        for dt in range(ND):
            op("vector", "tensor_tensor", xw[:, dt * TT:(dt + 1) * TT],
               xw[:, dt * TT:(dt + 1) * TT], rep1[:], op=MUL)
        drain("vector")
        for dt in range(ND):
            op("vector", "tensor_copy", outhi[:, dt * TT:(dt + 1) * TT],
               xw[:, dt * TT:(dt + 1) * TT])
        drain("vector")
        for dt in range(ND):
            op("vector", "tensor_tensor", outlo[:, dt * TT:(dt + 1) * TT],
               xw[:, dt * TT:(dt + 1) * TT], outhi[:, dt * TT:(dt + 1) * TT], op=SUB)
        mile("vector", sVE)

    def gating_masks(xhb, xlb, iL, iM):
        wait("tensor", sVE)
        wc("tensor", f"bk{iL}")
        first = True
        for dt in range(ND):
            for (xs, ws) in ((xhb, wgh), (xlb, wgh), (xhb, wgl)):
                last = (dt == ND - 1 and ws is wgl)
                op("tensor", "matmul", PS[iL][0:E, 0:TT],
                   ws[:, dt * E:(dt + 1) * E],
                   xs[:, dt * TT:(dt + 1) * TT],
                   start=first, stop=last, inc=sTE)
                first = False
        wait("vector", sTE)
        wc("vector", "lg8")
        op("vector", "tensor_copy", lg8[:], PS[iL][0:E, 0:TT])
        mile("vector", sVE)
        sc_(f"bk{iL}", sVE)
        wait("tensor", sVE)
        wc("tensor", f"bk{iM}")
        for c in range(NC4):
            op("tensor", "transpose", PS[iM][0:128, c * 8:(c + 1) * 8],
               lg8[:, c * 128:(c + 1) * 128], eye[0:E, 0:E], inc=sTE)
        sc_("lg8", sTE)
        wait("vector", sTE)
        lg3 = PS[iM][0:128, 0:NC4 * 8].rearrange("p (c e) -> p c e", c=NC4)
        op("vector", "tensor_reduce", rmax[:], lg3, op=MAXOP, axis=AX)
        op("vector", "memset", found[:], 0.0)
        drain("vector")
        for e_i in range(E):
            op("vector", "tensor_tensor", eqall[:, e_i * NC4:(e_i + 1) * NC4],
               lg3[:, :, e_i], rmax[:], op=ISEQ)
        drain("vector")
        for e_i in range(E):
            op("vector", "tensor_tensor", equ[:], eqall[:, e_i * NC4:(e_i + 1) * NC4], found[:], op=MUL)
            drain("vector")
            op("vector", "tensor_tensor", mtok[:, e_i * NC4:(e_i + 1) * NC4],
               eqall[:, e_i * NC4:(e_i + 1) * NC4], equ[:], op=SUB)
            drain("vector")
            op("vector", "tensor_tensor", found[:], found[:],
               mtok[:, e_i * NC4:(e_i + 1) * NC4], op=ADD)
            drain("vector")
        mile("vector", sVE)
        sc_(f"bk{iM}", sVE)
        mt3 = mtok[:].rearrange("p (e c) -> p c e", c=NC4)
        wait("tensor", sVE)
        wc("tensor", f"bk{iL}")
        for c in range(NC4):
            op("tensor", "transpose", PS[iL][0:E, c * 128:(c + 1) * 128],
               mt3[:, c, :], eye[:], inc=sTE)
        wait("vector", sTE)
        op("vector", "tensor_copy", masks8[:], PS[iL][0:E, 0:TT])
        mile("vector", sVE)
        sc_(f"bk{iL}", sVE)

    def ffn_tile(xap, bufs, y_consume):
        (w1b, w2b, gsp, xw, xhb, xlb, rep0, rep1, sq, gel, gmf, gdone) = bufs
        PARTS = cfg.get("parts", 63)
        if PARTS & 1:
            ln_stats_fused(xap, 4, 5, sq, double=True)
        if PARTS & 2:
            replicate_rows(4, 5, rep0, rep1)
        if PARTS & 4:
            affine_split_nd(xap, rep0, rep1, xw, xhb, xlb)
        if PARTS & 8:
            gating_masks(xhb, xlb, 2, 3)
        masks_n = sVE.n
        for e_i in range(E if PARTS & 16 else 0):
            pw = e_i % 2
            wc("sync", f"w1b{pw}")
            wc("gpsimd", f"w1b{pw}")
            dma("sync", w1b[pw][0].rearrange("p (a h) -> p a h", h=HID), w1s[0, e_i, :, :].rearrange("(a p) h -> p a h", p=128), f"w1{pw}")
            dma("gpsimd", w1b[pw][1].rearrange("p (a h) -> p a h", h=HID), w1s[1, e_i, :, :].rearrange("(a p) h -> p a h", p=128), f"w1g{pw}")
            wait("tensor", DS(f"w1{pw}"))
            wait("tensor", DS(f"w1g{pw}"))
            te_hh = {}
            for hh in range(NH2):
                wc("tensor", f"bk{4 + hh}")
                first = True
                for dt in range(ND):
                    for (wb, xb) in ((0, xhb), (0, xlb), (1, xhb)):
                        last = (dt == ND - 1 and wb == 1)
                        op("tensor", "matmul", PS[4 + hh][0:128, 0:TT],
                           w1b[pw][wb][:, dt * HID + hh * 128:dt * HID + (hh + 1) * 128],
                           xb[:, dt * TT:(dt + 1) * TT], start=first, stop=last, inc=sTE)
                        first = False
                te_hh[hh] = sTE.n
            sc_(f"w1b{pw}", sTE)
            wait("tensor", sVE, masks_n)
            wc("tensor", "bk6")
            op("tensor", "matmul", PS[6][0:128, 0:TT],
               sel8[:, e_i * 128:(e_i + 1) * 128], masks8[:], start=True, stop=True, inc=sTE)
            mr = sTE.n
            for hh in range(NH2):
                wait("scalar", sTE, te_hh[hh])
                wc("scalar", "gel")
                op("scalar", "activation", gel[:, hh * TT:(hh + 1) * TT],
                   PS[4 + hh][0:128, 0:TT], Act.Gelu,
                   bias=b1eff[:, e_i * NH2 + hh:e_i * NH2 + hh + 1], inc=sSC)
                sc_(f"bk{4 + hh}", sSC)
            wait("vector", sSC)
            wait("vector", sTE, mr)
            for hh in range(NH2):
                g0 = (e_i * NH2 + hh) * 2 * TT
                op("vector", "scalar_tensor_tensor", gsp[:, g0:g0 + TT],
                   gel[:, hh * TT:(hh + 1) * TT], 1.0, PS[6][0:128, 0:TT], op0=MUL, op1=MUL)
                op("vector", "scalar_tensor_tensor", gmf[:, hh * TT:(hh + 1) * TT],
                   gel[:, hh * TT:(hh + 1) * TT], 1.0, PS[6][0:128, 0:TT], op0=MUL, op1=MUL)
            drain("vector")
            for hh in range(NH2):
                g0 = (e_i * NH2 + hh) * 2 * TT
                op("vector", "tensor_tensor", gsp[:, g0 + TT:g0 + 2 * TT],
                   gmf[:, hh * TT:(hh + 1) * TT], gsp[:, g0:g0 + TT], op=SUB)
            mile("vector", sVE)
            sc_("bk6", sVE)
            sc_("gel", sVE)
            gdone[e_i] = sVE.n
        sc_("sq", sVE)
        for dt in range(ND if PARTS & 32 else 0):
            pb2 = dt % 2
            wc("sync", f"w2d{pb2}")
            wc("gpsimd", f"w2d{pb2}")
            dma("sync", w2b[pb2][0].rearrange("p (e c) -> p e c", c=128),
                w2s[0, :, :, dt * 128:(dt + 1) * 128].rearrange("e (h p) c -> p (e h) c", p=128), f"w2{pb2}")
            dma("gpsimd", w2b[pb2][1].rearrange("p (e c) -> p e c", c=128),
                w2s[1, :, :, dt * 128:(dt + 1) * 128].rearrange("e (h p) c -> p (e h) c", p=128), f"w2g{pb2}")
            wait("tensor", sVE, masks_n)
            wc("tensor", f"bk{pb2}")
            op("tensor", "matmul", PS[pb2][0:128, 0:TT],
               b2eff[:, dt * 128:(dt + 1) * 128], masks8[:], start=True, stop=False, inc=sTE)
            wait("tensor", sVE, gdone[E - 1])
            wait("tensor", DS(f"w2{pb2}"))
            wait("tensor", DS(f"w2g{pb2}"))
            for e_i in range(E):
                for hh in range(NH2):
                    g0 = (e_i * NH2 + hh) * 2 * TT
                    w0 = (e_i * NH2 + hh) * 128
                    for (wb, go) in ((0, 0), (0, TT), (1, 0)):
                        last = (e_i == E - 1 and hh == NH2 - 1 and (wb, go) == (1, 0))
                        op("tensor", "matmul", PS[pb2][0:128, 0:TT],
                           w2b[pb2][wb][:, w0:w0 + 128],
                           gsp[:, g0 + go:g0 + go + TT], start=False, stop=last, inc=sTE)
            sc_(f"w2d{pb2}", sTE)
            y_consume(dt, pb2)

    A.reset()
    w1b = [(A.bf16(128, ND * HID), A.bf16(128, ND * HID)) for _ in range(2)]
    w2b = [(A.bf16(128, E * NH2 * 128), A.bf16(128, E * NH2 * 128)) for _ in range(2)]
    gsp = A.bf16(128, E * NH2 * TT * 2)
    xa = A.f32(128, ND * TT)
    xhb = A.bf16(128, ND * TT)
    xlb = A.bf16(128, ND * TT)
    rep0 = A.f32(128, TT)
    rep1 = A.f32(128, TT)
    sq = A.f32(128, 4 * TT)
    gel = sq[:, 0:NH2 * TT]
    gmf = sq[:, NH2 * TT:2 * NH2 * TT]
    xw = gsp[:, 0:2 * ND * TT].bitcast(F32)
    gdone = [0] * E
    bufs1 = (w1b, w2b, gsp, xw, xhb, xlb, rep0, rep1, sq, gel, gmf, gdone)

    for ti in range(NTF):
        ts = slice(ti * TT, (ti + 1) * TT)
        wc("sync", "xa")
        dma("sync", xa[:].rearrange("p (a t) -> p a t", a=ND), x_view[:, :, ts], "xa")
        wait("vector", DS("xa")); wait("scalar", DS("xa")); wait("tensor", DS("xa"))

        def y1_consume(dt, dt4, ts=ts):
            wait("vector", sTE)
            if cfg.get("p1_y", 1):
                op("vector", "tensor_tensor", xa[:, dt * TT:(dt + 1) * TT],
                   xa[:, dt * TT:(dt + 1) * TT], PS[dt4][0:128, 0:TT], op=ADD)
            else:
                op("vector", "tensor_scalar", xa[:, dt * TT:(dt + 1) * TT],
                   xa[:, dt * TT:(dt + 1) * TT], 1.0, None, op0=MUL)
            mile("vector", sVE)
            sc_(f"bk{dt4}", sVE)
            wait("gpsimd", sVE)
            dma("gpsimd", x1sp[:, ts].rearrange("(a p) t -> p a t", p=128)[:, dt, :],
                xa[:, dt * TT:(dt + 1) * TT], "st_x")

        ffn_tile(xa, bufs1, y1_consume)
        if not (cfg.get("parts", 63) & 32):
            mile("vector", sVE)
            wait("gpsimd", sVE)
            for dt in range(ND):
                dma("gpsimd", x1sp[:, ts].rearrange("(a p) t -> p a t", p=128)[:, dt, :],
                    xa[:, dt * TT:(dt + 1) * TT], "st_x")
        sc_("xa", DS("st_x"))
    wait("gpsimd", DS("st_x"))
    barrier()
    cons.clear()
    maybe_stop(1)

    A.reset()
    xqh = A.bf16(128, ND * SC)
    xql = A.bf16(128, ND * SC)
    xa = A.f32(128, ND * TT)
    rep0 = A.f32(128, TT)
    rep1 = A.f32(128, TT)
    sq = A.f32(128, 4 * TT)
    xw = A.f32(128, ND * TT)
    wqb = [(A.bf16(128, ND * 128), A.bf16(128, ND * 128)) for _ in range(2)]
    stg_h = [A.bf16(128, TT) for _ in range(2)]
    stg_l = [A.bf16(128, TT) for _ in range(2)]
    stg_b = A.bf16(128, NC4 * 64)
    sqn = [A.f32(128, TT) for _ in range(2)]
    v2 = [A.f32(128, NC4 * 128) for _ in range(2)]
    v2r = [v.bitcast(F32R) for v in v2]

    op("vector", "memset", v2[0].rearrange("p (a c) -> p a c", c=128)[:, :, 127:128], 1.0)
    op("vector", "memset", v2[1].rearrange("p (a c) -> p a c", c=128)[:, :, 127:128], 1.0)
    mile("vector", sVE)

    xqh3 = xqh[:].rearrange("p (a t) -> p a t", a=ND)
    xql3 = xql[:].rearrange("p (a t) -> p a t", a=ND)

    nrm_pend = [None]
    nrm_ct = [0]

    def nrm_flush():
        if nrm_pend[0] is None:
            return
        sg_, kindp, tcol, sq_n = nrm_pend[0]
        nrm_pend[0] = None
        wait("tensor", sSC, sq_n)
        wc("tensor", "bk6n")
        op("tensor", "matmul", PS[6][0:2, 0:TT], hsel[:], sqn[sg_][:],
           start=True, stop=True, inc=sTE)
        sc_(f"sqn{sg_}", sTE)
        nrm_mm = sTE.n
        wait("vector", sTE, nrm_mm)
        tabv = kn_tab if kindp == "k" else qn_tab
        op("vector", "tensor_reduce", nrm2[0:2, sg_:sg_ + 1], PS[6][0:2, 0:TT],
           op=MAXOP, axis=AX)
        drain("vector")
        op("vector", "tensor_tensor", tabv[0:2, tcol:tcol + 1],
           tabv[0:2, tcol:tcol + 1], nrm2[0:2, sg_:sg_ + 1], op=MAXOP)
        mile("vector", sVE)
        sc_("bk6n", sVE)

    for sc in range(NSC):
        for tj in range(NTSC):
            ti = sc * NTSC + tj
            ts = slice(ti * TT, (ti + 1) * TT)
            wc("sync", "xa")
            dma("sync", xa[:].rearrange("p (a t) -> p a t", a=ND), x1_view[:, :, ts], "xa")
            wait("vector", DS("xa")); wait("scalar", DS("xa")); wait("tensor", DS("xa"))
            wc("vector", "xq")
            ln_stats_fused(xa, 4, 5, sq, double=False)
            replicate_rows(4, 5, rep0, rep1)
            for dt in range(ND):
                op("vector", "tensor_tensor", xw[:, dt * TT:(dt + 1) * TT],
                   xa[:, dt * TT:(dt + 1) * TT], rep0[:], op=SUB)
            drain("vector")
            for dt in range(ND):
                op("vector", "tensor_tensor", xw[:, dt * TT:(dt + 1) * TT],
                   xw[:, dt * TT:(dt + 1) * TT], rep1[:], op=MUL)
            drain("vector")
            for dt in range(ND):
                op("vector", "tensor_copy", xqh3[:, dt, tj * TT:(tj + 1) * TT],
                   xw[:, dt * TT:(dt + 1) * TT])
            drain("vector")
            for dt in range(ND):
                op("vector", "tensor_tensor", xql3[:, dt, tj * TT:(tj + 1) * TT],
                   xw[:, dt * TT:(dt + 1) * TT], xqh3[:, dt, tj * TT:(tj + 1) * TT], op=SUB)
            mile("vector", sVE)
            sc_("xa", sVE)
        ot_list = list(range(NDH, NO3)) + (list(range(0, NDH)) if sc * SC < TO else [])
        kinds_on = cfg.get("p2_kinds", "qkv")
        ot_list = [o for o in ot_list if ("q" if o < NDH else ("k" if o < 2 * NDH else "v")) in kinds_on]
        for oi, ot in enumerate(ot_list):
            kind = "q" if ot < NDH else ("k" if ot < 2 * NDH else "v")
            pw = oi % 2
            wc("sync", f"wqb{pw}")
            wc("gpsimd", f"wqb{pw}")
            dma("sync", wqb[pw][0].rearrange("p (a c) -> p a c", c=128), wqkvs[0, :, ot * 128:(ot + 1) * 128].rearrange("(a p) c -> p a c", p=128), f"wq{pw}")
            dma("gpsimd", wqb[pw][1].rearrange("p (a c) -> p a c", c=128), wqkvs[1, :, ot * 128:(ot + 1) * 128].rearrange("(a p) c -> p a c", p=128), f"wqg{pw}")
            wait("tensor", DS(f"wq{pw}"))
            wait("tensor", DS(f"wqg{pw}"))
            wait("tensor", sVE)
            tg_list = [tg for tg in range(NTSC)
                       if kind != "q" or (sc * SC + tg * TT) < TO]
            for tg in range(NTSC):
                if tg not in tg_list:
                    continue
                bi = 4 + (tg % 2)
                wc("tensor", f"bk{bi}")
                if kind == "v":
                    for c in range(NC4):
                        first = True
                        for dt in range(ND):
                            for (wb, xb3) in ((0, xqh3), (1, xqh3), (0, xql3)):
                                last = (dt == ND - 1 and wb == 0 and xb3 is xql3)
                                op("tensor", "matmul", PS[bi][0:128, c * 128:(c + 1) * 128],
                                   xb3[:, dt, tg * TT + c * 128:tg * TT + (c + 1) * 128],
                                   wqb[pw][wb][:, dt * 128:(dt + 1) * 128],
                                   start=first, stop=last, inc=sTE)
                                first = False
                else:
                    first = True
                    for dt in range(ND):
                        for (wb, xb3) in ((0, xqh3), (0, xql3), (1, xqh3)):
                            last = (dt == ND - 1 and wb == 1)
                            op("tensor", "matmul", PS[bi][0:128, 0:TT],
                               wqb[pw][wb][:, dt * 128:(dt + 1) * 128],
                               xb3[:, dt, tg * TT:(tg + 1) * TT], start=first, stop=last, inc=sTE)
                            first = False
                sc_(f"wqb{pw}", sTE)
                sc_("xq", sTE)
                te_mm = sTE.n
                gtok = sc * SC + tg * TT
                tsg = slice(gtok, gtok + TT)
                tso = slice(gtok, gtok + TT)
                if kind in ("k", "q"):
                    h2 = (ot - NDH) * 2 if kind == "k" else ot * 2
                    sg = tg % 2
                    nci = nrm_ct[0] % 2
                    nrm_ct[0] += 1
                    wait("scalar", sTE, te_mm)
                    wc("scalar", f"sqn{nci}")
                    op("scalar", "activation", sqn[nci][:], PS[bi][0:128, 0:TT],
                       Act.Square, inc=sSC)
                    sq_n = sSC.n
                    nrm_flush()
                    nrm_pend[0] = (nci, kind, (ot - NDH) if kind == "k" else ot, sq_n)
                    wait("vector", sTE, te_mm)
                    wait("vector", sSC, sq_n)
                    wc("vector", f"stg{sg}")
                    op("vector", "tensor_copy", stg_h[sg][:], PS[bi][0:128, 0:TT])
                    drain("vector")
                    op("vector", "tensor_tensor", stg_l[sg][:], PS[bi][0:128, 0:TT], stg_h[sg][:], op=SUB)
                    mile("vector", sVE)
                    sc_(f"bk{bi}", sVE)
                    wait("gpsimd", sVE)
                    dst = ksp if kind == "k" else qsp
                    dsl = tsg if kind == "k" else tso
                    for hh in range(2):
                        hd = h2 + hh
                        dma("gpsimd", dst[hd, 0:64, dsl], stg_h[sg][hh * 64:(hh + 1) * 64, :], f"st_stg{sg}")
                        dma("gpsimd", dst[hd, 64:128, dsl], stg_l[sg][hh * 64:(hh + 1) * 64, :], f"st_stg{sg}")
                    sc_(f"stg{sg}", DS(f"st_stg{sg}"))
                else:
                    h2 = (ot - 2 * NDH) * 2
                    ps3 = PS[bi].rearrange("p (a c) -> p a c", c=128)
                    for hh in range(2):
                        hd = h2 + hh
                        vg = hh % 2
                        v23 = v2[vg].rearrange("p (a c) -> p a c", c=128)
                        v23r = v2r[vg].rearrange("p (a c) -> p a c", c=128)
                        wait("vector", sTE, te_mm)
                        wc("vector", f"v2w{vg}")
                        for c in range(NC4):
                            op("vector", "tensor_copy", stg_b[:, c * 64:(c + 1) * 64],
                               ps3[:, c, hh * 64:hh * 64 + 64])
                        drain("vector")
                        for c in range(NC4):
                            op("vector", "tensor_copy", v23r[:, c, 0:64], stg_b[:, c * 64:(c + 1) * 64])
                        drain("vector")
                        for c in range(NC4):
                            op("vector", "tensor_tensor", v23r[:, c, 64:127],
                               ps3[:, c, hh * 64:hh * 64 + 63], v23[:, c, 0:63], op=SUB)
                        mile("vector", sVE)
                        wait("gpsimd", sVE)
                        dma("gpsimd", vsp[hd, gtok:gtok + TT, :].rearrange("(a p) c -> p a c", p=128),
                            v23r[:], f"st_v2{vg}")
                        sc_(f"v2w{vg}", DS(f"st_v2{vg}"))
                    sc_(f"bk{bi}", sVE)
    nrm_flush()
    wait("gpsimd", DS("st_stg0"))
    wait("gpsimd", DS("st_stg1"))
    wait("gpsimd", DS("st_v20"))
    wait("gpsimd", DS("st_v21"))
    barrier()
    cons.clear()
    maybe_stop(2)

    A.reset()
    khl = [A.bf16(128, TF) for _ in range(2)]
    k1 = [A.bf16(65, TF) for _ in range(2)]
    qq_h = A.bf16(128, TO)
    qq_lm = A.bf16(65, TO)
    ofin = A.f32(64, TT)
    olh = A.bf16(64, TT)
    oll = A.bf16(64, TT)
    ngm = A.f32(65, 1)
    r_qm = A.f32(1, 1)

    op("vector", "memset", k1[0][64:65, :], 1.0)
    op("vector", "memset", k1[1][64:65, :], 1.0)
    mile("vector", sVE)

    def load_head(hd, pb):
        wc("scalar", f"khl{pb}")
        dma("scalar", khl[pb], ksp[hd, :, :], f"hd{pb}")
        dma("scalar", k1[pb][0:64, :], ksp[hd, 0:64, :], f"hd{pb}")
        return DS(f"hd{pb}").n

    NVG = (NKT + 7) // 8
    NVG_S = (NVG + 1) // 2

    def load_v(hd):
        wc("sync", "vq")
        wc("gpsimd", "vq")
        vv3 = vqr[:].rearrange("p (a c) -> p a c", c=128)
        for g in range(NVG):
            c0 = g * 8
            cc = min(8, NKT - c0)
            eng, key = ("sync", "vq") if g < NVG_S else ("gpsimd", "vqg")
            dma(eng, vv3[:, c0:c0 + cc, :],
                vsp[hd, c0 * 128:(c0 + cc) * 128, :].rearrange("(a p) c -> p a c", p=128), key)
        return DS("vq").n, DS("vqg").n

    fin_pend = [None]
    fin_rec = [0]

    def fin_stageB():
        if fin_pend[0] is None:
            return
        hd_, qs_, avsb_n = fin_pend[0]
        wait("tensor", sVE, avsb_n)
        wc("tensor", "bk6f")
        op("tensor", "matmul", PS[6][0:128, 0:TT], msel[:], avsbr[:],
           start=True, stop=True, inc=sTE)
        sc_("avsb", sTE)
        te_cmb = sTE.n
        wait("vector", sTE, te_cmb)
        wc("vector", "rec65")
        op("vector", "reciprocal", rec65[64:65, :], PS[6][64:65, 0:TT])
        mile("vector", sVE)
        fin_rec[0] = sVE.n

    def fin_stageC():
        if fin_pend[0] is None:
            return
        hd_, qs_, _ = fin_pend[0]
        fin_pend[0] = None
        wait("tensor", sVE, fin_rec[0])
        wc("tensor", "bk7b")
        op("tensor", "matmul", PS[7][0:64, 0:TT], onesq[64:65, 0:64],
           rec65[64:65, :], start=True, stop=True, inc=sTE)
        te_rep = sTE.n
        wait("vector", sTE, te_rep)
        wc("vector", "fin")
        op("vector", "tensor_copy", ofin[:], PS[6][0:64, 0:TT])
        drain("vector")
        op("vector", "tensor_tensor", ofin[:], ofin[:], PS[7][0:64, 0:TT], op=MUL)
        drain("vector")
        op("vector", "tensor_copy", olh[:], ofin[:])
        drain("vector")
        op("vector", "tensor_tensor", oll[:], ofin[:], olh[:], op=SUB)
        mile("vector", sVE)
        sc_("bk6f", sVE)
        sc_("bk7b", sVE)
        sc_("rec65", sVE)
        wait("gpsimd", sVE)
        dma("gpsimd", aosp[0, hd_ * 64:(hd_ + 1) * 64, qs_], olh[:], "st_ao")
        dma("gpsimd", aosp[1, hd_ * 64:(hd_ + 1) * 64, qs_], oll[:], "st_ao")
        sc_("fin", DS("st_ao"))

    ld_marks = {0: load_head(0, 0)}
    for hd in range(H):
        pb = hd % 2
        wc("sync", "qt")
        dma("sync", qq_h[0:64, :], qsp[hd, 0:64, :], "qld")
        dma("sync", qq_h[64:128, :], qsp[hd, 0:64, :], "qld")
        dma("sync", qq_lm[0:64, :], qsp[hd, 64:128, :], "qld")
        pp = hd % 2
        pc = hd // 2
        op("vector", "tensor_scalar", nrm2[0:2, 0:1],
           qn_tab[0:2, pc:pc + 1], kn_tab[0:2, pc:pc + 1],
           1.0201, op0=MUL, op1=MUL)
        mile("vector", sVE)
        wait("scalar", sVE)
        op("scalar", "activation", nrm2[0:2, 1:2], nrm2[0:2, 0:1],
           Act.Sqrt, inc=sSC)
        wait("vector", sSC)
        op("vector", "tensor_scalar", ngm[0:2, 0:1], nrm2[0:2, 1:2],
           -1.0, None, op0=MUL)
        mile("vector", sVE)
        wait("sync", sVE)
        dma("sync", ngm[64:65, 0:1], ngm[pp:pp + 1, 0:1], "ngm")
        wait("vector", DS("ngm"))
        wait("vector", DS("qld"))
        op("vector", "tensor_scalar", qq_lm[64:65, :], k1[pb][64:65, 0:TO],
           ngm[64:65, 0:1], None, op0=MUL)
        mile("vector", sVE)
        qrow_n = sVE.n
        vmark = load_v(hd)
        if hd + 1 < H:
            ld_marks[hd + 1] = load_head(hd + 1, (hd + 1) % 2)
        wait("tensor", DS(f"hd{pb}"), ld_marks[hd])
        for qg in range(NQG):
            qs = slice(qg * TT, (qg + 1) * TT)
            wait("tensor", sVE, qrow_n)
            ab = 4 + (qg % 2)
            wc("tensor", f"bkav{qg % 2}")
            vv3 = vqr[:].rearrange("p (a c) -> p a c", c=128)
            NP = NKT // 2
            exp_mark = {}

            def emit_av(p, ab=ab, vv3=vv3, qg=qg, vmark=vmark):
                if qg == 0:
                    g = (2 * p + 1) // 8
                    if g < NVG_S:
                        wait("tensor", DS("vq"), vmark[0] - 16 * NVG_S + 16 * (g + 1))
                    else:
                        wait("tensor", DS("vqg"),
                             vmark[1] - 16 * (NVG - NVG_S) + 16 * (g - NVG_S + 1))
                wait("tensor", sSC, exp_mark[p])
                for h in range(2):
                    kt = 2 * p + h
                    op("tensor", "matmul", PS[ab][0:128, 0:TT],
                       vv3[:, kt, :], p_sr[:, (p % 2) * 2 * TT + h * TT:(p % 2) * 2 * TT + (h + 1) * TT],
                       start=(kt == 0), stop=(kt == NKT - 1), inc=sTE)
                sc_(f"pp{p % 2}", sTE)

            for p in range(NP):
                sp = p % 2
                wc("tensor", f"spair{sp}")
                for h in range(2):
                    kt = 2 * p + h
                    ksl = slice(kt * 128, (kt + 1) * 128)
                    sap = PSALL[0:128, sp * 2 * TT + h * TT:sp * 2 * TT + (h + 1) * TT]
                    op("tensor", "matmul", sap, khl[pb][:, ksl], qq_h[:, qs],
                       start=True, stop=False, inc=sTE)
                    op("tensor", "matmul", sap, k1[pb][:, ksl], qq_lm[:, qs],
                       start=False, stop=True, inc=sTE)
                te_qk = sTE.n
                wait("scalar", sTE, te_qk)
                wc("scalar", f"pp{sp}")
                op("scalar", "activation", p_sr[:, sp * 2 * TT:(sp + 1) * 2 * TT],
                   PSALL[0:128, sp * 2 * TT:(sp + 1) * 2 * TT], Act.Exp, inc=sSC)
                sc_(f"spair{sp}", sSC)
                exp_mark[p] = sSC.n
                if p > 0:
                    emit_av(p - 1)
                if p == 1:
                    fin_stageB()
                elif p == 3:
                    fin_stageC()
            emit_av(NP - 1)
            te_av = sTE.n
            sc_("qt", sTE)
            if qg == NQG - 1:
                sc_(f"khl{pb}", sTE)
                sc_("vq", sTE)
            wait("vector", sTE, te_av)
            wc("vector", "avsb")
            op("vector", "tensor_copy", avsbr[:], PS[ab][0:128, 0:TT])
            mile("vector", sVE)
            sc_(f"bkav{qg % 2}", sVE)
            fin_pend[0] = (hd, qs, sVE.n)
    fin_stageB()
    fin_stageC()
    wait("gpsimd", DS("st_ao"))
    barrier()
    cons.clear()
    maybe_stop(3)

    A.reset()
    w1b = [(A.bf16(128, ND * HID), A.bf16(128, ND * HID)) for _ in range(2)]
    w2b = [(A.bf16(128, E * NH2 * 128), A.bf16(128, E * NH2 * 128)) for _ in range(2)]
    gsp = A.bf16(128, E * NH2 * TT * 2)
    xa = A.f32(128, ND * TT)
    xhb = A.bf16(128, ND * TT)
    xlb = A.bf16(128, ND * TT)
    rep0 = A.f32(128, TT)
    rep1 = A.f32(128, TT)
    sq = A.f32(128, 4 * TT)
    gel = sq[:, 0:NH2 * TT]
    gmf = sq[:, NH2 * TT:2 * NH2 * TT]
    xw = gsp[:, 0:2 * ND * TT].bitcast(F32)
    aoh = A.bf16(128, NDH * TT)
    aol = A.bf16(128, NDH * TT)
    wpb = [(A.bf16(128, NDH * 128), A.bf16(128, NDH * 128))]
    gdone = [0] * E
    bufs4 = (w1b, w2b, gsp, xw, xhb, xlb, rep0, rep1, sq, gel, gmf, gdone)

    ao_h3 = aosp[0].rearrange("(a p) t -> p a t", p=128)
    ao_l3 = aosp[1].rearrange("(a p) t -> p a t", p=128)

    for ti in range(NTO):
        ts = slice(ti * TT, (ti + 1) * TT)
        wc("sync", "xa")
        dma("sync", xa[:].rearrange("p (a t) -> p a t", a=ND), x1_view[:, :, ts], "xa")
        wc("sync", "ao")
        dma("sync", aoh[:].rearrange("p (a t) -> p a t", a=NDH), ao_h3[:, :, ts], "ao")
        dma("sync", aol[:].rearrange("p (a t) -> p a t", a=NDH), ao_l3[:, :, ts], "ao")
        wait("tensor", DS("xa")); wait("tensor", DS("ao"))
        wait("vector", DS("xa")); wait("vector", DS("ao"))
        for dto in range(ND):
            pw = 0
            wc("sync", f"wpb{pw}")
            wc("gpsimd", f"wpb{pw}")
            dma("sync", wpb[pw][0].rearrange("p (a c) -> p a c", c=128), wprojs[0, :, dto * 128:(dto + 1) * 128].rearrange("(a p) c -> p a c", p=128), f"wp{pw}")
            dma("gpsimd", wpb[pw][1].rearrange("p (a c) -> p a c", c=128), wprojs[1, :, dto * 128:(dto + 1) * 128].rearrange("(a p) c -> p a c", p=128), f"wpg{pw}")
            bi = 4 + dto % 2
            wc("tensor", f"bk{bi}")
            wait("tensor", DS(f"wp{pw}"))
            wait("tensor", DS(f"wpg{pw}"))
            first = True
            for dt in range(NDH):
                for (wb, ab_) in ((0, aoh), (0, aol), (1, aoh)):
                    last = (dt == NDH - 1 and wb == 1)
                    op("tensor", "matmul", PS[bi][0:128, 0:TT],
                       wpb[pw][wb][:, dt * 128:(dt + 1) * 128],
                       ab_[:, dt * TT:(dt + 1) * TT], start=first, stop=last, inc=sTE)
                    first = False
            sc_(f"wpb{pw}", sTE)
            wait("vector", sTE)
            op("vector", "scalar_tensor_tensor", xa[:, dto * TT:(dto + 1) * TT],
               PS[bi][0:128, 0:TT], bproj[:, dto:dto + 1],
               xa[:, dto * TT:(dto + 1) * TT], op0=ADD, op1=ADD)
            mile("vector", sVE)
            sc_(f"bk{bi}", sVE)
        sc_("ao", sVE)
        wait("scalar", sVE)
        wait("tensor", sVE)

        def y2_consume(dt, dt4, ts=ts):
            wait("vector", sTE)
            op("vector", "tensor_tensor", xa[:, dt * TT:(dt + 1) * TT],
               xa[:, dt * TT:(dt + 1) * TT], PS[dt4][0:128, 0:TT], op=ADD)
            mile("vector", sVE)
            sc_(f"bk{dt4}", sVE)
            wait("gpsimd", sVE)
            dma("gpsimd", out_d[:, ts].rearrange("(a p) t -> p a t", p=128)[:, dt, :],
                xa[:, dt * TT:(dt + 1) * TT], "st_x")

        ffn_tile(xa, bufs4, y2_consume)
        sc_("xa", DS("st_x"))
    for kk in list(_DS) if False else []:
        pass
    wait("gpsimd", DS("st_x"))

    with nc.Block() as block:
        @block.sync
        def _(e):
            for f in streams["sync"]:
                f(e)

        @block.vector
        def _(e):
            for f in streams["vector"]:
                f(e)

        @block.scalar
        def _(e):
            for f in streams["scalar"]:
                f(e)

        @block.tensor
        def _(e):
            for f in streams["tensor"]:
                f(e)

        @block.gpsimd
        def _(e):
            for f in streams["gpsimd"]:
                f(e)
    es.close()
    return nc


def _make_msel():
    m = np.zeros((128, 128), dtype=np.float32)
    for c in range(64):
        m[c, c] = 1.0
        if c < 63:
            m[64 + c, c] = 1.0
    m[127, 64] = 1.0
    return m


def make_inputs_for_core(core, full):
    b = core // 2
    half = core % 2
    x = np.asarray(full["L0_attn_x"], dtype=np.float32)
    TF = x.shape[1]
    xb = x[b]
    if half == 1:
        xb = np.concatenate([xb[TF // 2:], xb[:TF // 2]], axis=0)
    E = 8
    sel8 = np.zeros((E, E * 128), dtype=np.float32)
    for e in range(E):
        sel8[e, e * 128:(e + 1) * 128] = 1.0
    return {
        "x_t": np.ascontiguousarray(xb.T),
        "w_qkv": np.ascontiguousarray(np.asarray(full["L0_attn_w_qkv"], dtype=np.float32)),
        "w_proj": np.ascontiguousarray(np.asarray(full["L0_attn_w_proj"], dtype=np.float32)),
        "b_proj": np.ascontiguousarray(np.asarray(full["L0_attn_b_proj"], dtype=np.float32).reshape(-1, 128).T),
        "wg_t": np.ascontiguousarray(np.asarray(full["L0_ffn_Wg"], dtype=np.float32).T),
        "ln_scale_t": np.ascontiguousarray(np.asarray(full["L0_ffn_ln_scale"], dtype=np.float32).T),
        "ln_bias_t": np.ascontiguousarray(np.asarray(full["L0_ffn_ln_bias"], dtype=np.float32).T),
        "w1": np.ascontiguousarray(np.asarray(full["L0_ffn_W1"], dtype=np.float32)),
        "b1": np.ascontiguousarray(np.asarray(full["L0_ffn_b1"], dtype=np.float32)),
        "w2": np.ascontiguousarray(np.asarray(full["L0_ffn_W2"], dtype=np.float32)),
        "b2": np.ascontiguousarray(np.asarray(full["L0_ffn_b2"], dtype=np.float32)),
        "eye_d": np.eye(128, dtype=np.float32),
        "msel_d": _make_msel(),
        "sel8_d": sel8,
    }


def assemble_output(results, B, TF, D):
    out = np.empty((B, TF, D), dtype=np.float32)
    for core, r in enumerate(results):
        b = core // 2
        half = core % 2
        o = np.asarray(r["out_t"]).T
        out[b, half * (TF // 2):(half + 1) * (TF // 2)] = o
    return out


_CFG = {"D": 1024, "TF": 8192, "TO": 4096, "H": 16}


def kernel(**inputs):
    from concourse.bass_utils import run_bass_kernel_spmd
    nc = build(_CFG)
    in_maps = [make_inputs_for_core(c, inputs) for c in range(8)]
    res = run_bass_kernel_spmd(nc, in_maps, core_ids=list(range(8)))
    return assemble_output(res.results, 4, _CFG["TF"], _CFG["D"])


# version 22
# speedup vs baseline: 1.1253x; 1.1253x over previous
import numpy as np
import concourse.bass as bass
import concourse.mybir as mybir

F32 = mybir.dt.float32
F32R = mybir.dt.float32r
BF16 = mybir.dt.bfloat16
ADD = mybir.AluOpType.add
SUB = mybir.AluOpType.subtract
MUL = mybir.AluOpType.mult
ISEQ = mybir.AluOpType.is_equal
MAXOP = mybir.AluOpType.max
Act = mybir.ActivationFunctionType
AX = mybir.AxisListType.X

ENGINES = ("sync", "vector", "scalar", "tensor", "gpsimd")


class Sem:
    def __init__(self, handle):
        self.h = handle
        self.n = 0


def build(cfg):
    D = cfg["D"]; TF = cfg["TF"]; TO = cfg["TO"]; H = cfg["H"]
    HD = 64; E = 8; HID = 256; TT = 512
    ND = D // 128
    NH2 = HID // 128
    NTF = TF // TT
    NTO = TO // TT
    NKT = TF // 128
    NQG = TO // TT
    DH = H * HD
    NDH = DH // 128
    NO3 = 3 * NDH
    SC = min(1024, TF)
    NSC = TF // SC
    NTSC = SC // TT
    NC4 = TT // 128
    LN_EPS = 1e-5
    assert D == 1024 and TO * 2 == TF and TT == 512

    nc = bass.Bass("TRN2", target_bir_lowering=False)

    x_d = nc.dram_tensor("x_t", [D, TF], F32, kind="ExternalInput").ap()
    wqkv_d = nc.dram_tensor("w_qkv", [D, 3 * DH], F32, kind="ExternalInput").ap()
    wproj_d = nc.dram_tensor("w_proj", [DH, D], F32, kind="ExternalInput").ap()
    bproj_d = nc.dram_tensor("b_proj", [128, ND], F32, kind="ExternalInput").ap()
    wgT_d = nc.dram_tensor("wg_t", [D, E], F32, kind="ExternalInput").ap()
    lnsT_d = nc.dram_tensor("ln_scale_t", [D, E], F32, kind="ExternalInput").ap()
    lnbT_d = nc.dram_tensor("ln_bias_t", [D, E], F32, kind="ExternalInput").ap()
    w1_d = nc.dram_tensor("w1", [E, D, HID], F32, kind="ExternalInput").ap()
    b1_d = nc.dram_tensor("b1", [E, HID], F32, kind="ExternalInput").ap()
    w2_d = nc.dram_tensor("w2", [E, HID, D], F32, kind="ExternalInput").ap()
    b2_d = nc.dram_tensor("b2", [E, D], F32, kind="ExternalInput").ap()
    eye_d = nc.dram_tensor("eye_d", [128, 128], F32, kind="ExternalInput").ap()
    msel_d = nc.dram_tensor("msel_d", [128, 128], F32, kind="ExternalInput").ap()
    sel8_d = nc.dram_tensor("sel8_d", [E, E * 128], F32, kind="ExternalInput").ap()
    out_d = nc.dram_tensor("out_t", [D, TO], F32, kind="ExternalOutput").ap()

    x1sp = nc.dram_tensor("x1sp", [D, TF], F32, kind=("ExternalOutput" if cfg.get("dbg") else "Internal")).ap()
    ksp = nc.dram_tensor("ksp", [H, 128, TF], BF16, kind="Internal").ap()
    vsp = nc.dram_tensor("vsp", [H, TF, 128], F32R, kind="Internal").ap()
    qsp = nc.dram_tensor("qsp", [H, 128, TO], BF16, kind="Internal").ap()
    aosp = nc.dram_tensor("aosp", [2, DH, TO], BF16, kind=("ExternalOutput" if cfg.get("dbg") else "Internal")).ap()
    wqkvs = nc.dram_tensor("wqkvs", [2, D, 3 * DH], BF16, kind="Internal").ap()
    wprojs = nc.dram_tensor("wprojs", [2, DH, D], BF16, kind="Internal").ap()
    w1s = nc.dram_tensor("w1s", [2, E, D, HID], BF16, kind="Internal").ap()
    w2s = nc.dram_tensor("w2s", [2, E, HID, D], BF16, kind="Internal").ap()

    from contextlib import ExitStack
    es = ExitStack()

    def sbt(name, shape, dt=F32):
        return es.enter_context(nc.sbuf_tensor(name, shape, dt))

    eye = sbt("eye", [128, 128])
    msel = sbt("msel", [128, 128], F32R)
    msel_f = sbt("msel_f", [128, 128])
    onesq = sbt("onesq", [65, 64])
    rec65 = sbt("rec65", [65, 512])
    sel8 = sbt("sel8", [E, E * 128])
    ones128 = sbt("ones128", [128, 1])
    ones1r = sbt("ones1r", [1, 128])
    onesb = sbt("onesb", [128, 1], BF16)
    bproj = sbt("bproj", [128, ND])
    wgT = sbt("wgT", [128, ND * E])
    lnsT = sbt("lnsT", [128, ND * E])
    lnbT = sbt("lnbT", [128, ND * E])
    b1eff = sbt("b1eff", [128, E * NH2])
    b2eff = sbt("b2eff", [E, D])
    st_s1 = sbt("st_s1", [1, TT]); st_s2 = sbt("st_s2", [1, TT])
    st_mu = sbt("st_mu", [1, TT]); st_v = sbt("st_v", [1, TT])
    st_t0 = sbt("st_t0", [1, TT]); st_t1 = sbt("st_t1", [1, TT])
    st_mc = sbt("st_mc", [1, TT]); st_rc = sbt("st_rc", [1, TT])
    rmax = sbt("rmax", [128, NC4]); found = sbt("found", [128, NC4])
    eqb = sbt("eqb", [128, NC4]); equ = sbt("equ", [128, NC4])
    eqall = sbt("eqall", [128, E * NC4])
    mtok = sbt("mtok", [128, NC4 * E])
    masks8 = sbt("masks8", [E, TT])
    qn_tab = sbt("qn_tab", [2, 8]); kn_tab = sbt("kn_tab", [2, 8])
    nrm2 = sbt("nrm2", [2, 2]); hsel = sbt("hsel", [128, 2])
    lg8 = sbt("lg8", [E, TT])
    eps1 = sbt("eps1", [1, 1])
    vqr = sbt("vqr", [128, TF], F32R)
    p_sr = sbt("p_sr", [128, 4 * TT], F32R)
    avsbr = sbt("avsbr", [128, TT], F32R)
    wgh = sbt("wgh", [128, ND * E], BF16)
    wgl = sbt("wgl", [128, ND * E], BF16)

    ARW = cfg.get("ARW", 65536)
    AR = sbt("AR", [128, ARW], BF16)

    class Arena:
        def __init__(self):
            self.off = 0

        def reset(self):
            self.off = 0

        def bf16(self, parts, w):
            o = (self.off + 3) & ~3
            self.off = o + w
            assert self.off <= ARW, f"arena overflow {self.off} > {ARW}"
            return AR[0:parts, o:o + w]

        def f32(self, parts, w):
            return self.bf16(parts, 2 * w).bitcast(F32)

    A = Arena()

    PSALL = es.enter_context(nc.psum_tensor("psall", [128, 4096], F32))
    PS = [PSALL[0:128, i * 512:(i + 1) * 512] for i in range(8)]

    _DS = {}

    def DS(key):
        if key not in _DS:
            _DS[key] = Sem(es.enter_context(nc.semaphore(f"d_{key}")))
        return _DS[key]
    sTE = Sem(es.enter_context(nc.semaphore("sTE")))
    sVE = Sem(es.enter_context(nc.semaphore("sVE")))
    sSC = Sem(es.enter_context(nc.semaphore("sSC")))
    sPH = Sem(es.enter_context(nc.semaphore("sPH")))

    streams = {e: [] for e in ENGINES}
    cons = {}
    STOP = cfg.get("stop", 9)

    class _StopEmission(Exception):
        pass

    stopped = [False]

    def maybe_stop(k):
        if STOP <= k:
            stopped[0] = True

    def op(eng, name, *args, inc=None, k=1, **kw):
        if stopped[0]:
            return
        if inc is not None:
            inc.n += k
            streams[eng].append(lambda e, n=name, a=args, kk=kw, h=inc.h, K=k:
                                getattr(e, n)(*a, **kk).then_inc(h, K))
        else:
            streams[eng].append(lambda e, n=name, a=args, kk=kw: getattr(e, n)(*a, **kk))

    def wait(eng, sem, val=None):
        if stopped[0]:
            return
        v = sem.n if val is None else val
        if v > 0:
            streams[eng].append(lambda e, h=sem.h, vv=v: e.wait_ge(h, vv))

    def mile(eng, sem, k=1):
        if stopped[0]:
            return
        sem.n += k
        streams[eng].append(lambda e, h=sem.h, kk=k: e.maybe_drain_then_inc((h, kk)))

    def drain(eng):
        if stopped[0]:
            return
        streams[eng].append(lambda e: e.drain())

    def dma(eng, out_ap, in_ap, key):
        if stopped[0]:
            return
        sem = DS(key)
        sem.n += 16
        streams[eng].append(lambda e, o=out_ap, i=in_ap, h=sem.h:
                            e.dma_start(out=o, in_=i).then_inc(h, 16))

    def wc(eng, key):
        if key in cons:
            s, v = cons[key]
            wait(eng, s, v)

    def sc_(key, sem):
        cons[key] = (sem, sem.n)

    def barrier():
        for e in ENGINES:
            mile(e, sPH, 1)
        tgt = sPH.n
        for e in ENGINES:
            wait(e, sPH, tgt)

    if cfg.get("skip_p0"):
        stopped[0] = True if False else stopped[0]
    _P0ON = not cfg.get("skip_p0")
    stopped[0] = not _P0ON
    dma("sync", eye[:], eye_d[:], "res")
    dma("sync", msel_f[:], msel_d[:], "res")
    dma("sync", sel8[:], sel8_d[:], "res")
    dma("sync", bproj[:], bproj_d[:], "res")
    for dt in range(ND):
        dma("sync", wgT[:, dt * E:(dt + 1) * E], wgT_d[dt * 128:(dt + 1) * 128, :], "res")
        dma("sync", lnsT[:, dt * E:(dt + 1) * E], lnsT_d[dt * 128:(dt + 1) * 128, :], "res")
        dma("sync", lnbT[:, dt * E:(dt + 1) * E], lnbT_d[dt * 128:(dt + 1) * 128, :], "res")
    dma("sync", b2eff[:], b2_d[:], "res")
    wait("vector", DS("res"))
    op("vector", "memset", ones128[:], 1.0)
    op("vector", "memset", ones1r[:], 1.0)
    op("vector", "memset", onesb[:], 1.0)
    op("vector", "memset", eps1[:], 1e-5)
    op("vector", "memset", onesq[:], 1.0)
    op("vector", "memset", qn_tab[:], 0.0)
    op("vector", "memset", kn_tab[:], 0.0)
    op("vector", "memset", hsel[:], 0.0)
    drain("vector")
    op("vector", "memset", hsel[0:64, 0:1], 1.0)
    op("vector", "memset", hsel[64:128, 1:2], 1.0)
    op("vector", "tensor_scalar", b2eff[:], b2eff[:], 0.5, None, op0=MUL)
    op("vector", "tensor_copy", msel[:], msel_f[:])
    op("vector", "tensor_copy", wgh[:], wgT[:])
    drain("vector")
    op("vector", "tensor_tensor", wgl[:], wgT[:], wgh[:], op=SUB)
    mile("vector", sVE)

    A.reset()
    wt = [A.f32(128, 1024) for _ in range(2)]
    whi = [A.bf16(128, 1024) for _ in range(2)]
    wlo = [A.bf16(128, 1024) for _ in range(2)]
    w1f = [A.f32(128, HID) for _ in range(2)]
    b1t = A.f32(128, E * NH2)
    p0c = [0]

    def p0j():
        j = p0c[0] % 2
        p0c[0] += 1
        return j

    def split_cols(j, src_ap, cols):
        op("vector", "tensor_copy", whi[j][:, 0:cols], src_ap)
        drain("vector")
        op("vector", "tensor_tensor", wlo[j][:, 0:cols], src_ap, whi[j][:, 0:cols], op=SUB)
        mile("vector", sVE)

    P0S = cfg.get("p0_sec", 31)
    CW = 1024
    for dt in range(ND if P0S & 1 else 0):
        for c0 in range(0, 3 * DH, CW):
            cols = min(CW, 3 * DH - c0)
            j = p0j()
            wc("sync", f"wt{j}")
            dma("sync", wt[j][:, 0:cols], wqkv_d[dt * 128:(dt + 1) * 128, c0:c0 + cols], f"wt{j}")
            wait("vector", DS(f"wt{j}"))
            wc("vector", f"whilo{j}")
            qc = min(max(DH - c0, 0), cols)
            if qc > 0:
                op("vector", "tensor_scalar", wt[j][:, 0:qc], wt[j][:, 0:qc], 0.125, None, op0=MUL)
                drain("vector")
            split_cols(j, wt[j][:, 0:cols], cols)
            sc_(f"wt{j}", sVE)
            wait("gpsimd", sVE)
            dma("gpsimd", wqkvs[0, dt * 128:(dt + 1) * 128, c0:c0 + cols], whi[j][:, 0:cols], f"st_w{j}")
            dma("gpsimd", wqkvs[1, dt * 128:(dt + 1) * 128, c0:c0 + cols], wlo[j][:, 0:cols], f"st_w{j}")
            sc_(f"whilo{j}", DS(f"st_w{j}"))
    for dt in range(NDH if P0S & 2 else 0):
        j = p0j()
        wc("sync", f"wt{j}")
        dma("sync", wt[j][:, 0:D], wproj_d[dt * 128:(dt + 1) * 128, :], f"wt{j}")
        wait("vector", DS(f"wt{j}"))
        wc("vector", f"whilo{j}")
        split_cols(j, wt[j][:, 0:D], D)
        sc_(f"wt{j}", sVE)
        wait("gpsimd", sVE)
        dma("gpsimd", wprojs[0, dt * 128:(dt + 1) * 128, :], whi[j][:, 0:D], f"st_w{j}")
        dma("gpsimd", wprojs[1, dt * 128:(dt + 1) * 128, :], wlo[j][:, 0:D], f"st_w{j}")
        sc_(f"whilo{j}", DS(f"st_w{j}"))
    for e_i in range(E if P0S & 4 else 0):
        for dt in range(ND):
            j = p0j()
            wc("sync", f"wt{j}")
            dma("sync", wt[j][:, 0:HID], w1_d[e_i, dt * 128:(dt + 1) * 128, :], f"wt{j}")
            wait("vector", DS(f"wt{j}"))
            wc("vector", f"whilo{j}")
            wc("vector", f"w1f{j}")
            op("vector", "tensor_scalar", w1f[j][:], wt[j][:, 0:HID],
               lnsT[:, dt * E + e_i:dt * E + e_i + 1], None, op0=MUL)
            drain("vector")
            split_cols(j, w1f[j][:], HID)
            sc_(f"wt{j}", sVE)
            wait("gpsimd", sVE)
            dma("gpsimd", w1s[0, e_i, dt * 128:(dt + 1) * 128, :], whi[j][:, 0:HID], f"st_w{j}")
            dma("gpsimd", w1s[1, e_i, dt * 128:(dt + 1) * 128, :], wlo[j][:, 0:HID], f"st_w{j}")
            sc_(f"whilo{j}", DS(f"st_w{j}"))
            wait("tensor", sVE)
            for hh in range(NH2):
                op("tensor", "matmul",
                   PS[0][0:128, (e_i * NH2 + hh) * ND + dt:(e_i * NH2 + hh) * ND + dt + 1],
                   w1f[j][:, hh * 128:(hh + 1) * 128],
                   lnbT[:, dt * E + e_i:dt * E + e_i + 1],
                   start=True, stop=True, inc=sTE)
            sc_(f"w1f{j}", sTE)
    for e_i in range(E if P0S & 4 else 0):
        for hh in range(NH2):
            dma("sync", b1t[:, e_i * NH2 + hh:e_i * NH2 + hh + 1],
                b1_d[e_i:e_i + 1, hh * 128:(hh + 1) * 128], "b1t")
    if P0S & 4:
        wait("vector", DS("b1t"))
        wait("vector", sTE)
        op("vector", "tensor_reduce", b1eff[:],
           PS[0][0:128, 0:E * NH2 * ND].rearrange("p (c d) -> p c d", d=ND),
           op=ADD, axis=AX)
        drain("vector")
        op("vector", "tensor_tensor", b1eff[:], b1eff[:], b1t[:], op=ADD)
        mile("vector", sVE)
    for e_i in range(E if P0S & 8 else 0):
        for hh in range(NH2):
            j = p0j()
            wc("sync", f"wt{j}")
            dma("sync", wt[j][:, 0:D], w2_d[e_i, hh * 128:(hh + 1) * 128, :], f"wt{j}")
            wait("vector", DS(f"wt{j}"))
            wc("vector", f"whilo{j}")
            op("vector", "tensor_scalar", wt[j][:, 0:D], wt[j][:, 0:D], 0.5, None, op0=MUL)
            drain("vector")
            split_cols(j, wt[j][:, 0:D], D)
            sc_(f"wt{j}", sVE)
            wait("gpsimd", sVE)
            dma("gpsimd", w2s[0, e_i, hh * 128:(hh + 1) * 128, :], whi[j][:, 0:D], f"st_w{j}")
            dma("gpsimd", w2s[1, e_i, hh * 128:(hh + 1) * 128, :], wlo[j][:, 0:D], f"st_w{j}")
            sc_(f"whilo{j}", DS(f"st_w{j}"))
    stopped[0] = False
    [wait("gpsimd", DS(f"st_w{jj}")) for jj in range(2) if _P0ON and DS(f"st_w{jj}").n > 0]
    barrier()
    cons.clear()
    maybe_stop(0)

    x_view = x_d.rearrange("(a p) t -> p a t", p=128)
    x1_view = x1sp.rearrange("(a p) t -> p a t", p=128)

    def ln_stats_fused(xap, iA, iB, sq, double):
        wc("tensor", f"bk{iA}")
        for dt in range(ND):
            op("tensor", "matmul", PS[iA][0:1, 0:TT], ones128[:],
               xap[:, dt * TT:(dt + 1) * TT],
               start=(dt == 0), stop=(dt == ND - 1), inc=sTE)
        wc("tensor", f"bk{iB}")
        for h4 in range(2):
            wc("scalar", "sq")
            for d4 in range(4):
                dt = h4 * 4 + d4
                op("scalar", "activation", sq[:, d4 * TT:(d4 + 1) * TT],
                   xap[:, dt * TT:(dt + 1) * TT], Act.Square, inc=sSC)
            wait("tensor", sSC)
            for d4 in range(4):
                dt = h4 * 4 + d4
                op("tensor", "matmul", PS[iB][0:1, 0:TT], ones128[:],
                   sq[:, d4 * TT:(d4 + 1) * TT],
                   start=(dt == 0), stop=(dt == ND - 1), inc=sTE)
            sc_("sq", sTE)
        wait("vector", sTE)
        op("vector", "tensor_scalar", st_mu[:], PS[iA][0:1, 0:TT], 1.0 / D, None, op0=MUL)
        op("vector", "tensor_scalar", st_s2[:], PS[iB][0:1, 0:TT], 1.0 / D, None, op0=MUL)
        drain("vector")
        op("vector", "tensor_tensor", st_t0[:], st_mu[:], st_mu[:], op=MUL)
        drain("vector")
        op("vector", "tensor_tensor", st_v[:], st_s2[:], st_t0[:], op=SUB)
        mile("vector", sVE)
        sc_(f"bk{iA}", sVE); sc_(f"bk{iB}", sVE)
        wait("scalar", sVE)
        op("scalar", "activation", st_t1[:], st_v[:], Act.Sqrt, bias=eps1[0:1, 0:1], inc=sSC)
        wait("vector", sSC)
        op("vector", "reciprocal", st_t0[:], st_t1[:])
        drain("vector")
        if not double:
            op("vector", "tensor_copy", st_rc[:], st_t0[:])
            op("vector", "tensor_copy", st_mc[:], st_mu[:])
            mile("vector", sVE)
            return
        op("vector", "tensor_tensor", st_s2[:], st_t0[:], st_t0[:], op=MUL)
        drain("vector")
        op("vector", "tensor_tensor", st_s1[:], st_v[:], st_s2[:], op=MUL)
        mile("vector", sVE)
        wait("scalar", sVE)
        op("scalar", "activation", st_t1[:], st_s1[:], Act.Sqrt, bias=eps1[0:1, 0:1], inc=sSC)
        wait("vector", sSC)
        op("vector", "reciprocal", st_s2[:], st_t1[:])
        drain("vector")
        op("vector", "tensor_tensor", st_rc[:], st_t0[:], st_s2[:], op=MUL)
        op("vector", "tensor_copy", st_mc[:], st_mu[:])
        mile("vector", sVE)

    def replicate_rows(iA, iB, rep0, rep1):
        wait("tensor", sVE)
        wc("tensor", f"bk{iA}"); wc("tensor", f"bk{iB}")
        op("tensor", "matmul", PS[iA][0:128, 0:TT], ones1r[:], st_mc[:], start=True, stop=True, inc=sTE)
        op("tensor", "matmul", PS[iB][0:128, 0:TT], ones1r[:], st_rc[:], start=True, stop=True, inc=sTE)
        wait("vector", sTE)
        op("vector", "tensor_copy", rep0[:], PS[iA][0:128, 0:TT])
        op("vector", "tensor_copy", rep1[:], PS[iB][0:128, 0:TT])
        mile("vector", sVE)
        sc_(f"bk{iA}", sVE); sc_(f"bk{iB}", sVE)

    def affine_split_nd(xap, rep0, rep1, xw, outhi, outlo):
        for dt in range(ND):
            op("vector", "tensor_tensor", xw[:, dt * TT:(dt + 1) * TT],
               xap[:, dt * TT:(dt + 1) * TT], rep0[:], op=SUB)
        drain("vector")
        for dt in range(ND):
            op("vector", "tensor_tensor", xw[:, dt * TT:(dt + 1) * TT],
               xw[:, dt * TT:(dt + 1) * TT], rep1[:], op=MUL)
        drain("vector")
        for dt in range(ND):
            op("vector", "tensor_copy", outhi[:, dt * TT:(dt + 1) * TT],
               xw[:, dt * TT:(dt + 1) * TT])
        drain("vector")
        for dt in range(ND):
            op("vector", "tensor_tensor", outlo[:, dt * TT:(dt + 1) * TT],
               xw[:, dt * TT:(dt + 1) * TT], outhi[:, dt * TT:(dt + 1) * TT], op=SUB)
        mile("vector", sVE)

    def gating_masks(xhb, xlb, iL, iM):
        wait("tensor", sVE)
        wc("tensor", f"bk{iL}")
        first = True
        for dt in range(ND):
            for (xs, ws) in ((xhb, wgh), (xlb, wgh), (xhb, wgl)):
                last = (dt == ND - 1 and ws is wgl)
                op("tensor", "matmul", PS[iL][0:E, 0:TT],
                   ws[:, dt * E:(dt + 1) * E],
                   xs[:, dt * TT:(dt + 1) * TT],
                   start=first, stop=last, inc=sTE)
                first = False
        wait("vector", sTE)
        wc("vector", "lg8")
        op("vector", "tensor_copy", lg8[:], PS[iL][0:E, 0:TT])
        mile("vector", sVE)
        sc_(f"bk{iL}", sVE)
        wait("tensor", sVE)
        wc("tensor", f"bk{iM}")
        for c in range(NC4):
            op("tensor", "transpose", PS[iM][0:128, c * 8:(c + 1) * 8],
               lg8[:, c * 128:(c + 1) * 128], eye[0:E, 0:E], inc=sTE)
        sc_("lg8", sTE)
        wait("vector", sTE)
        lg3 = PS[iM][0:128, 0:NC4 * 8].rearrange("p (c e) -> p c e", c=NC4)
        op("vector", "tensor_reduce", rmax[:], lg3, op=MAXOP, axis=AX)
        op("vector", "memset", found[:], 0.0)
        drain("vector")
        for e_i in range(E):
            op("vector", "tensor_tensor", eqall[:, e_i * NC4:(e_i + 1) * NC4],
               lg3[:, :, e_i], rmax[:], op=ISEQ)
        drain("vector")
        for e_i in range(E):
            op("vector", "tensor_tensor", equ[:], eqall[:, e_i * NC4:(e_i + 1) * NC4], found[:], op=MUL)
            drain("vector")
            op("vector", "tensor_tensor", mtok[:, e_i * NC4:(e_i + 1) * NC4],
               eqall[:, e_i * NC4:(e_i + 1) * NC4], equ[:], op=SUB)
            drain("vector")
            op("vector", "tensor_tensor", found[:], found[:],
               mtok[:, e_i * NC4:(e_i + 1) * NC4], op=ADD)
            drain("vector")
        mile("vector", sVE)
        sc_(f"bk{iM}", sVE)
        mt3 = mtok[:].rearrange("p (e c) -> p c e", c=NC4)
        wait("tensor", sVE)
        wc("tensor", f"bk{iL}")
        for c in range(NC4):
            op("tensor", "transpose", PS[iL][0:E, c * 128:(c + 1) * 128],
               mt3[:, c, :], eye[:], inc=sTE)
        wait("vector", sTE)
        op("vector", "tensor_copy", masks8[:], PS[iL][0:E, 0:TT])
        mile("vector", sVE)
        sc_(f"bk{iL}", sVE)

    def ffn_tile(xap, bufs, y_consume):
        (w1b, w2b, gsp, xw, xhb, xlb, rep0, rep1, sq, gel, gmf, gdone) = bufs
        PARTS = cfg.get("parts", 63)
        if PARTS & 1:
            ln_stats_fused(xap, 4, 5, sq, double=True)
        if PARTS & 2:
            replicate_rows(4, 5, rep0, rep1)
        if PARTS & 4:
            affine_split_nd(xap, rep0, rep1, xw, xhb, xlb)
        if PARTS & 8:
            gating_masks(xhb, xlb, 2, 3)
        masks_n = sVE.n
        for e_i in range(E if PARTS & 16 else 0):
            pw = e_i % 2
            wc("sync", f"w1b{pw}")
            dma("sync", w1b[pw][0].rearrange("p (a h) -> p a h", h=HID), w1s[0, e_i, :, :].rearrange("(a p) h -> p a h", p=128), f"w1{pw}")
            dma("sync", w1b[pw][1].rearrange("p (a h) -> p a h", h=HID), w1s[1, e_i, :, :].rearrange("(a p) h -> p a h", p=128), f"w1{pw}")
            wait("tensor", DS(f"w1{pw}"))
            te_hh = {}
            for hh in range(NH2):
                wc("tensor", f"bk{4 + hh}")
                first = True
                for dt in range(ND):
                    for (wb, xb) in ((0, xhb), (0, xlb), (1, xhb)):
                        last = (dt == ND - 1 and wb == 1)
                        op("tensor", "matmul", PS[4 + hh][0:128, 0:TT],
                           w1b[pw][wb][:, dt * HID + hh * 128:dt * HID + (hh + 1) * 128],
                           xb[:, dt * TT:(dt + 1) * TT], start=first, stop=last, inc=sTE)
                        first = False
                te_hh[hh] = sTE.n
            sc_(f"w1b{pw}", sTE)
            wait("tensor", sVE, masks_n)
            wc("tensor", "bk6")
            op("tensor", "matmul", PS[6][0:128, 0:TT],
               sel8[:, e_i * 128:(e_i + 1) * 128], masks8[:], start=True, stop=True, inc=sTE)
            mr = sTE.n
            for hh in range(NH2):
                wait("scalar", sTE, te_hh[hh])
                wc("scalar", "gel")
                op("scalar", "activation", gel[:, hh * TT:(hh + 1) * TT],
                   PS[4 + hh][0:128, 0:TT], Act.Gelu,
                   bias=b1eff[:, e_i * NH2 + hh:e_i * NH2 + hh + 1], inc=sSC)
                sc_(f"bk{4 + hh}", sSC)
            wait("vector", sSC)
            wait("vector", sTE, mr)
            for hh in range(NH2):
                g0 = (e_i * NH2 + hh) * 2 * TT
                op("vector", "scalar_tensor_tensor", gsp[:, g0:g0 + TT],
                   gel[:, hh * TT:(hh + 1) * TT], 1.0, PS[6][0:128, 0:TT], op0=MUL, op1=MUL)
                op("vector", "scalar_tensor_tensor", gmf[:, hh * TT:(hh + 1) * TT],
                   gel[:, hh * TT:(hh + 1) * TT], 1.0, PS[6][0:128, 0:TT], op0=MUL, op1=MUL)
            drain("vector")
            for hh in range(NH2):
                g0 = (e_i * NH2 + hh) * 2 * TT
                op("vector", "tensor_tensor", gsp[:, g0 + TT:g0 + 2 * TT],
                   gmf[:, hh * TT:(hh + 1) * TT], gsp[:, g0:g0 + TT], op=SUB)
            mile("vector", sVE)
            sc_("bk6", sVE)
            sc_("gel", sVE)
            gdone[e_i] = sVE.n
        sc_("sq", sVE)
        for dt in range(ND if PARTS & 32 else 0):
            pb2 = dt % 2
            wc("sync", f"w2d{pb2}")
            dma("sync", w2b[pb2][0].rearrange("p (e c) -> p e c", c=128),
                w2s[0, :, :, dt * 128:(dt + 1) * 128].rearrange("e (h p) c -> p (e h) c", p=128), f"w2{pb2}")
            dma("sync", w2b[pb2][1].rearrange("p (e c) -> p e c", c=128),
                w2s[1, :, :, dt * 128:(dt + 1) * 128].rearrange("e (h p) c -> p (e h) c", p=128), f"w2{pb2}")
            wait("tensor", sVE, masks_n)
            wc("tensor", f"bk{pb2}")
            op("tensor", "matmul", PS[pb2][0:128, 0:TT],
               b2eff[:, dt * 128:(dt + 1) * 128], masks8[:], start=True, stop=False, inc=sTE)
            wait("tensor", sVE, gdone[E - 1])
            wait("tensor", DS(f"w2{pb2}"))
            for e_i in range(E):
                for hh in range(NH2):
                    g0 = (e_i * NH2 + hh) * 2 * TT
                    w0 = (e_i * NH2 + hh) * 128
                    for (wb, go) in ((0, 0), (0, TT), (1, 0)):
                        last = (e_i == E - 1 and hh == NH2 - 1 and (wb, go) == (1, 0))
                        op("tensor", "matmul", PS[pb2][0:128, 0:TT],
                           w2b[pb2][wb][:, w0:w0 + 128],
                           gsp[:, g0 + go:g0 + go + TT], start=False, stop=last, inc=sTE)
            sc_(f"w2d{pb2}", sTE)
            y_consume(dt, pb2)

    A.reset()
    w1b = [(A.bf16(128, ND * HID), A.bf16(128, ND * HID)) for _ in range(2)]
    w2b = [(A.bf16(128, E * NH2 * 128), A.bf16(128, E * NH2 * 128)) for _ in range(2)]
    gsp = A.bf16(128, E * NH2 * TT * 2)
    xa = A.f32(128, ND * TT)
    xhb = A.bf16(128, ND * TT)
    xlb = A.bf16(128, ND * TT)
    rep0 = A.f32(128, TT)
    rep1 = A.f32(128, TT)
    sq = A.f32(128, 4 * TT)
    gel = sq[:, 0:NH2 * TT]
    gmf = sq[:, NH2 * TT:2 * NH2 * TT]
    xw = gsp[:, 0:2 * ND * TT].bitcast(F32)
    gdone = [0] * E
    bufs1 = (w1b, w2b, gsp, xw, xhb, xlb, rep0, rep1, sq, gel, gmf, gdone)

    for ti in range(NTF):
        ts = slice(ti * TT, (ti + 1) * TT)
        wc("sync", "xa")
        dma("sync", xa[:].rearrange("p (a t) -> p a t", a=ND), x_view[:, :, ts], "xa")
        wait("vector", DS("xa")); wait("scalar", DS("xa")); wait("tensor", DS("xa"))

        def y1_consume(dt, dt4, ts=ts):
            wait("vector", sTE)
            if cfg.get("p1_y", 1):
                op("vector", "tensor_tensor", xa[:, dt * TT:(dt + 1) * TT],
                   xa[:, dt * TT:(dt + 1) * TT], PS[dt4][0:128, 0:TT], op=ADD)
            else:
                op("vector", "tensor_scalar", xa[:, dt * TT:(dt + 1) * TT],
                   xa[:, dt * TT:(dt + 1) * TT], 1.0, None, op0=MUL)
            mile("vector", sVE)
            sc_(f"bk{dt4}", sVE)
            wait("gpsimd", sVE)
            dma("gpsimd", x1sp[:, ts].rearrange("(a p) t -> p a t", p=128)[:, dt, :],
                xa[:, dt * TT:(dt + 1) * TT], "st_x")

        ffn_tile(xa, bufs1, y1_consume)
        if not (cfg.get("parts", 63) & 32):
            mile("vector", sVE)
            wait("gpsimd", sVE)
            for dt in range(ND):
                dma("gpsimd", x1sp[:, ts].rearrange("(a p) t -> p a t", p=128)[:, dt, :],
                    xa[:, dt * TT:(dt + 1) * TT], "st_x")
        sc_("xa", DS("st_x"))
    wait("gpsimd", DS("st_x"))
    barrier()
    cons.clear()
    maybe_stop(1)

    A.reset()
    xqh = A.bf16(128, ND * SC)
    xql = A.bf16(128, ND * SC)
    xa = A.f32(128, ND * TT)
    rep0 = A.f32(128, TT)
    rep1 = A.f32(128, TT)
    sq = A.f32(128, 4 * TT)
    xw = A.f32(128, ND * TT)
    wqb = [(A.bf16(128, ND * 128), A.bf16(128, ND * 128)) for _ in range(2)]
    stg_h = [A.bf16(128, TT) for _ in range(2)]
    stg_l = [A.bf16(128, TT) for _ in range(2)]
    stg_b = A.bf16(128, NC4 * 64)
    sqn = [A.f32(128, TT) for _ in range(2)]
    v2 = [A.f32(128, NC4 * 128) for _ in range(2)]
    v2r = [v.bitcast(F32R) for v in v2]

    op("vector", "memset", v2[0].rearrange("p (a c) -> p a c", c=128)[:, :, 127:128], 1.0)
    op("vector", "memset", v2[1].rearrange("p (a c) -> p a c", c=128)[:, :, 127:128], 1.0)
    mile("vector", sVE)

    xqh3 = xqh[:].rearrange("p (a t) -> p a t", a=ND)
    xql3 = xql[:].rearrange("p (a t) -> p a t", a=ND)

    nrm_pend = [None]
    nrm_ct = [0]

    def nrm_flush():
        if nrm_pend[0] is None:
            return
        sg_, kindp, tcol, sq_n = nrm_pend[0]
        nrm_pend[0] = None
        wait("tensor", sSC, sq_n)
        wc("tensor", "bk6n")
        op("tensor", "matmul", PS[6][0:2, 0:TT], hsel[:], sqn[sg_][:],
           start=True, stop=True, inc=sTE)
        sc_(f"sqn{sg_}", sTE)
        nrm_mm = sTE.n
        wait("vector", sTE, nrm_mm)
        tabv = kn_tab if kindp == "k" else qn_tab
        op("vector", "tensor_reduce", nrm2[0:2, sg_:sg_ + 1], PS[6][0:2, 0:TT],
           op=MAXOP, axis=AX)
        drain("vector")
        op("vector", "tensor_tensor", tabv[0:2, tcol:tcol + 1],
           tabv[0:2, tcol:tcol + 1], nrm2[0:2, sg_:sg_ + 1], op=MAXOP)
        mile("vector", sVE)
        sc_("bk6n", sVE)

    for sc in range(NSC):
        for tj in range(NTSC):
            ti = sc * NTSC + tj
            ts = slice(ti * TT, (ti + 1) * TT)
            wc("sync", "xa")
            dma("sync", xa[:].rearrange("p (a t) -> p a t", a=ND), x1_view[:, :, ts], "xa")
            wait("vector", DS("xa")); wait("scalar", DS("xa")); wait("tensor", DS("xa"))
            wc("vector", "xq")
            ln_stats_fused(xa, 4, 5, sq, double=False)
            replicate_rows(4, 5, rep0, rep1)
            for dt in range(ND):
                op("vector", "tensor_tensor", xw[:, dt * TT:(dt + 1) * TT],
                   xa[:, dt * TT:(dt + 1) * TT], rep0[:], op=SUB)
            drain("vector")
            for dt in range(ND):
                op("vector", "tensor_tensor", xw[:, dt * TT:(dt + 1) * TT],
                   xw[:, dt * TT:(dt + 1) * TT], rep1[:], op=MUL)
            drain("vector")
            for dt in range(ND):
                op("vector", "tensor_copy", xqh3[:, dt, tj * TT:(tj + 1) * TT],
                   xw[:, dt * TT:(dt + 1) * TT])
            drain("vector")
            for dt in range(ND):
                op("vector", "tensor_tensor", xql3[:, dt, tj * TT:(tj + 1) * TT],
                   xw[:, dt * TT:(dt + 1) * TT], xqh3[:, dt, tj * TT:(tj + 1) * TT], op=SUB)
            mile("vector", sVE)
            sc_("xa", sVE)
        ot_list = list(range(NDH, NO3)) + (list(range(0, NDH)) if sc * SC < TO else [])
        kinds_on = cfg.get("p2_kinds", "qkv")
        ot_list = [o for o in ot_list if ("q" if o < NDH else ("k" if o < 2 * NDH else "v")) in kinds_on]
        for oi, ot in enumerate(ot_list):
            kind = "q" if ot < NDH else ("k" if ot < 2 * NDH else "v")
            pw = oi % 2
            wc("sync", f"wqb{pw}")
            dma("sync", wqb[pw][0].rearrange("p (a c) -> p a c", c=128), wqkvs[0, :, ot * 128:(ot + 1) * 128].rearrange("(a p) c -> p a c", p=128), f"wq{pw}")
            dma("sync", wqb[pw][1].rearrange("p (a c) -> p a c", c=128), wqkvs[1, :, ot * 128:(ot + 1) * 128].rearrange("(a p) c -> p a c", p=128), f"wq{pw}")
            wait("tensor", DS(f"wq{pw}"))
            wait("tensor", sVE)
            tg_list = [tg for tg in range(NTSC)
                       if kind != "q" or (sc * SC + tg * TT) < TO]
            for tg in range(NTSC):
                if tg not in tg_list:
                    continue
                bi = 4 + (tg % 2)
                wc("tensor", f"bk{bi}")
                if kind == "v":
                    for c in range(NC4):
                        first = True
                        for dt in range(ND):
                            for (wb, xb3) in ((0, xqh3), (1, xqh3), (0, xql3)):
                                last = (dt == ND - 1 and wb == 0 and xb3 is xql3)
                                op("tensor", "matmul", PS[bi][0:128, c * 128:(c + 1) * 128],
                                   xb3[:, dt, tg * TT + c * 128:tg * TT + (c + 1) * 128],
                                   wqb[pw][wb][:, dt * 128:(dt + 1) * 128],
                                   start=first, stop=last, inc=sTE)
                                first = False
                else:
                    first = True
                    for dt in range(ND):
                        for (wb, xb3) in ((0, xqh3), (0, xql3), (1, xqh3)):
                            last = (dt == ND - 1 and wb == 1)
                            op("tensor", "matmul", PS[bi][0:128, 0:TT],
                               wqb[pw][wb][:, dt * 128:(dt + 1) * 128],
                               xb3[:, dt, tg * TT:(tg + 1) * TT], start=first, stop=last, inc=sTE)
                            first = False
                sc_(f"wqb{pw}", sTE)
                sc_("xq", sTE)
                te_mm = sTE.n
                gtok = sc * SC + tg * TT
                tsg = slice(gtok, gtok + TT)
                tso = slice(gtok, gtok + TT)
                if kind in ("k", "q"):
                    h2 = (ot - NDH) * 2 if kind == "k" else ot * 2
                    sg = tg % 2
                    nci = nrm_ct[0] % 2
                    nrm_ct[0] += 1
                    wait("scalar", sTE, te_mm)
                    wc("scalar", f"sqn{nci}")
                    op("scalar", "activation", sqn[nci][:], PS[bi][0:128, 0:TT],
                       Act.Square, inc=sSC)
                    sq_n = sSC.n
                    nrm_flush()
                    nrm_pend[0] = (nci, kind, (ot - NDH) if kind == "k" else ot, sq_n)
                    wait("vector", sTE, te_mm)
                    wait("vector", sSC, sq_n)
                    wc("vector", f"stg{sg}")
                    op("vector", "tensor_copy", stg_h[sg][:], PS[bi][0:128, 0:TT])
                    drain("vector")
                    op("vector", "tensor_tensor", stg_l[sg][:], PS[bi][0:128, 0:TT], stg_h[sg][:], op=SUB)
                    mile("vector", sVE)
                    sc_(f"bk{bi}", sVE)
                    wait("gpsimd", sVE)
                    dst = ksp if kind == "k" else qsp
                    dsl = tsg if kind == "k" else tso
                    for hh in range(2):
                        hd = h2 + hh
                        dma("gpsimd", dst[hd, 0:64, dsl], stg_h[sg][hh * 64:(hh + 1) * 64, :], f"st_stg{sg}")
                        dma("gpsimd", dst[hd, 64:128, dsl], stg_l[sg][hh * 64:(hh + 1) * 64, :], f"st_stg{sg}")
                    sc_(f"stg{sg}", DS(f"st_stg{sg}"))
                else:
                    h2 = (ot - 2 * NDH) * 2
                    ps3 = PS[bi].rearrange("p (a c) -> p a c", c=128)
                    for hh in range(2):
                        hd = h2 + hh
                        vg = hh % 2
                        v23 = v2[vg].rearrange("p (a c) -> p a c", c=128)
                        v23r = v2r[vg].rearrange("p (a c) -> p a c", c=128)
                        wait("vector", sTE, te_mm)
                        wc("vector", f"v2w{vg}")
                        for c in range(NC4):
                            op("vector", "tensor_copy", stg_b[:, c * 64:(c + 1) * 64],
                               ps3[:, c, hh * 64:hh * 64 + 64])
                        drain("vector")
                        for c in range(NC4):
                            op("vector", "tensor_copy", v23r[:, c, 0:64], stg_b[:, c * 64:(c + 1) * 64])
                        drain("vector")
                        for c in range(NC4):
                            op("vector", "tensor_tensor", v23r[:, c, 64:127],
                               ps3[:, c, hh * 64:hh * 64 + 63], v23[:, c, 0:63], op=SUB)
                        mile("vector", sVE)
                        wait("gpsimd", sVE)
                        dma("gpsimd", vsp[hd, gtok:gtok + TT, :].rearrange("(a p) c -> p a c", p=128),
                            v23r[:], f"st_v2{vg}")
                        sc_(f"v2w{vg}", DS(f"st_v2{vg}"))
                    sc_(f"bk{bi}", sVE)
    nrm_flush()
    wait("gpsimd", DS("st_stg0"))
    wait("gpsimd", DS("st_stg1"))
    wait("gpsimd", DS("st_v20"))
    wait("gpsimd", DS("st_v21"))
    barrier()
    cons.clear()
    maybe_stop(2)

    A.reset()
    khl = [A.bf16(128, TF) for _ in range(2)]
    k1 = [A.bf16(65, TF) for _ in range(2)]
    qq_h = A.bf16(128, TO)
    qq_lm = A.bf16(65, TO)
    ofin = A.f32(64, TT)
    olh = A.bf16(64, TT)
    oll = A.bf16(64, TT)
    ngm = A.f32(65, 1)
    r_qm = A.f32(1, 1)

    op("vector", "memset", k1[0][64:65, :], 1.0)
    op("vector", "memset", k1[1][64:65, :], 1.0)
    mile("vector", sVE)

    def load_head(hd, pb):
        wc("scalar", f"khl{pb}")
        dma("scalar", khl[pb], ksp[hd, :, :], f"hd{pb}")
        dma("scalar", k1[pb][0:64, :], ksp[hd, 0:64, :], f"hd{pb}")
        return DS(f"hd{pb}").n

    NVG = (NKT + 7) // 8
    NVG_S = (NVG + 1) // 2

    def load_v(hd):
        wc("sync", "vq")
        wc("gpsimd", "vq")
        vv3 = vqr[:].rearrange("p (a c) -> p a c", c=128)
        for g in range(NVG):
            c0 = g * 8
            cc = min(8, NKT - c0)
            eng, key = ("sync", "vq") if g < NVG_S else ("gpsimd", "vqg")
            dma(eng, vv3[:, c0:c0 + cc, :],
                vsp[hd, c0 * 128:(c0 + cc) * 128, :].rearrange("(a p) c -> p a c", p=128), key)
        return DS("vq").n, DS("vqg").n

    fin_pend = [None]
    fin_rec = [0]

    def fin_stageB():
        if fin_pend[0] is None:
            return
        hd_, qs_, avsb_n = fin_pend[0]
        wait("tensor", sVE, avsb_n)
        wc("tensor", "bk6f")
        op("tensor", "matmul", PS[6][0:128, 0:TT], msel[:], avsbr[:],
           start=True, stop=True, inc=sTE)
        sc_("avsb", sTE)
        te_cmb = sTE.n
        wait("vector", sTE, te_cmb)
        wc("vector", "rec65")
        op("vector", "reciprocal", rec65[64:65, :], PS[6][64:65, 0:TT])
        mile("vector", sVE)
        fin_rec[0] = sVE.n

    def fin_stageC():
        if fin_pend[0] is None:
            return
        hd_, qs_, _ = fin_pend[0]
        fin_pend[0] = None
        wait("tensor", sVE, fin_rec[0])
        wc("tensor", "bk7b")
        op("tensor", "matmul", PS[7][0:64, 0:TT], onesq[64:65, 0:64],
           rec65[64:65, :], start=True, stop=True, inc=sTE)
        te_rep = sTE.n
        wait("vector", sTE, te_rep)
        wc("vector", "fin")
        op("vector", "tensor_copy", ofin[:], PS[6][0:64, 0:TT])
        drain("vector")
        op("vector", "tensor_tensor", ofin[:], ofin[:], PS[7][0:64, 0:TT], op=MUL)
        drain("vector")
        op("vector", "tensor_copy", olh[:], ofin[:])
        drain("vector")
        op("vector", "tensor_tensor", oll[:], ofin[:], olh[:], op=SUB)
        mile("vector", sVE)
        sc_("bk6f", sVE)
        sc_("bk7b", sVE)
        sc_("rec65", sVE)
        wait("gpsimd", sVE)
        dma("gpsimd", aosp[0, hd_ * 64:(hd_ + 1) * 64, qs_], olh[:], "st_ao")
        dma("gpsimd", aosp[1, hd_ * 64:(hd_ + 1) * 64, qs_], oll[:], "st_ao")
        sc_("fin", DS("st_ao"))

    ld_marks = {0: load_head(0, 0)}
    for hd in range(H):
        pb = hd % 2
        wc("sync", "qt")
        dma("sync", qq_h[0:64, :], qsp[hd, 0:64, :], "qld")
        dma("sync", qq_h[64:128, :], qsp[hd, 0:64, :], "qld")
        dma("sync", qq_lm[0:64, :], qsp[hd, 64:128, :], "qld")
        pp = hd % 2
        pc = hd // 2
        op("vector", "tensor_scalar", nrm2[0:2, 0:1],
           qn_tab[0:2, pc:pc + 1], kn_tab[0:2, pc:pc + 1],
           1.0201, op0=MUL, op1=MUL)
        mile("vector", sVE)
        wait("scalar", sVE)
        op("scalar", "activation", nrm2[0:2, 1:2], nrm2[0:2, 0:1],
           Act.Sqrt, inc=sSC)
        wait("vector", sSC)
        op("vector", "tensor_scalar", ngm[0:2, 0:1], nrm2[0:2, 1:2],
           -1.0, None, op0=MUL)
        mile("vector", sVE)
        wait("sync", sVE)
        dma("sync", ngm[64:65, 0:1], ngm[pp:pp + 1, 0:1], "ngm")
        wait("vector", DS("ngm"))
        wait("vector", DS("qld"))
        op("vector", "tensor_scalar", qq_lm[64:65, :], k1[pb][64:65, 0:TO],
           ngm[64:65, 0:1], None, op0=MUL)
        mile("vector", sVE)
        qrow_n = sVE.n
        vmark = load_v(hd)
        if hd + 1 < H:
            ld_marks[hd + 1] = load_head(hd + 1, (hd + 1) % 2)
        wait("tensor", DS(f"hd{pb}"), ld_marks[hd])
        for qg in range(NQG):
            qs = slice(qg * TT, (qg + 1) * TT)
            wait("tensor", sVE, qrow_n)
            ab = 4 + (qg % 2)
            wc("tensor", f"bkav{qg % 2}")
            vv3 = vqr[:].rearrange("p (a c) -> p a c", c=128)
            NP = NKT // 2
            exp_mark = {}

            def emit_av(p, ab=ab, vv3=vv3, qg=qg, vmark=vmark):
                if qg == 0:
                    g = (2 * p + 1) // 8
                    if g < NVG_S:
                        wait("tensor", DS("vq"), vmark[0] - 16 * NVG_S + 16 * (g + 1))
                    else:
                        wait("tensor", DS("vqg"),
                             vmark[1] - 16 * (NVG - NVG_S) + 16 * (g - NVG_S + 1))
                wait("tensor", sSC, exp_mark[p])
                for h in range(2):
                    kt = 2 * p + h
                    op("tensor", "matmul", PS[ab][0:128, 0:TT],
                       vv3[:, kt, :], p_sr[:, (p % 2) * 2 * TT + h * TT:(p % 2) * 2 * TT + (h + 1) * TT],
                       start=(kt == 0), stop=(kt == NKT - 1), inc=sTE)
                sc_(f"pp{p % 2}", sTE)

            for p in range(NP):
                sp = p % 2
                wc("tensor", f"spair{sp}")
                for h in range(2):
                    kt = 2 * p + h
                    ksl = slice(kt * 128, (kt + 1) * 128)
                    sap = PSALL[0:128, sp * 2 * TT + h * TT:sp * 2 * TT + (h + 1) * TT]
                    op("tensor", "matmul", sap, khl[pb][:, ksl], qq_h[:, qs],
                       start=True, stop=False, inc=sTE)
                    op("tensor", "matmul", sap, k1[pb][:, ksl], qq_lm[:, qs],
                       start=False, stop=True, inc=sTE)
                te_qk = sTE.n
                wait("scalar", sTE, te_qk)
                wc("scalar", f"pp{sp}")
                op("scalar", "activation", p_sr[:, sp * 2 * TT:(sp + 1) * 2 * TT],
                   PSALL[0:128, sp * 2 * TT:(sp + 1) * 2 * TT], Act.Exp, inc=sSC)
                sc_(f"spair{sp}", sSC)
                exp_mark[p] = sSC.n
                if p > 0:
                    emit_av(p - 1)
                if p == 1:
                    fin_stageB()
                elif p == 3:
                    fin_stageC()
            emit_av(NP - 1)
            te_av = sTE.n
            sc_("qt", sTE)
            if qg == NQG - 1:
                sc_(f"khl{pb}", sTE)
                sc_("vq", sTE)
            wait("vector", sTE, te_av)
            wc("vector", "avsb")
            op("vector", "tensor_copy", avsbr[:], PS[ab][0:128, 0:TT])
            mile("vector", sVE)
            sc_(f"bkav{qg % 2}", sVE)
            fin_pend[0] = (hd, qs, sVE.n)
    fin_stageB()
    fin_stageC()
    wait("gpsimd", DS("st_ao"))
    barrier()
    cons.clear()
    maybe_stop(3)

    A.reset()
    w1b = [(A.bf16(128, ND * HID), A.bf16(128, ND * HID)) for _ in range(2)]
    w2b = [(A.bf16(128, E * NH2 * 128), A.bf16(128, E * NH2 * 128)) for _ in range(2)]
    gsp = A.bf16(128, E * NH2 * TT * 2)
    xa = A.f32(128, ND * TT)
    xhb = A.bf16(128, ND * TT)
    xlb = A.bf16(128, ND * TT)
    rep0 = A.f32(128, TT)
    rep1 = A.f32(128, TT)
    sq = A.f32(128, 4 * TT)
    gel = sq[:, 0:NH2 * TT]
    gmf = sq[:, NH2 * TT:2 * NH2 * TT]
    xw = gsp[:, 0:2 * ND * TT].bitcast(F32)
    aoh = A.bf16(128, NDH * TT)
    aol = A.bf16(128, NDH * TT)
    wpb = [(A.bf16(128, NDH * 128), A.bf16(128, NDH * 128)),
           (w1b[1][0][:, 0:NDH * 128], w1b[1][1][:, 0:NDH * 128])]
    gdone = [0] * E
    bufs4 = (w1b, w2b, gsp, xw, xhb, xlb, rep0, rep1, sq, gel, gmf, gdone)

    ao_h3 = aosp[0].rearrange("(a p) t -> p a t", p=128)
    ao_l3 = aosp[1].rearrange("(a p) t -> p a t", p=128)

    for ti in range(NTO):
        ts = slice(ti * TT, (ti + 1) * TT)
        wc("sync", "xa")
        dma("sync", xa[:].rearrange("p (a t) -> p a t", a=ND), x1_view[:, :, ts], "xa")
        wc("sync", "ao")
        dma("sync", aoh[:].rearrange("p (a t) -> p a t", a=NDH), ao_h3[:, :, ts], "ao")
        dma("sync", aol[:].rearrange("p (a t) -> p a t", a=NDH), ao_l3[:, :, ts], "ao")
        wait("tensor", DS("xa")); wait("tensor", DS("ao"))
        wait("vector", DS("xa")); wait("vector", DS("ao"))
        for dto in range(ND):
            pw = dto % 2
            wc("sync", f"wpb{pw}")
            dma("sync", wpb[pw][0].rearrange("p (a c) -> p a c", c=128), wprojs[0, :, dto * 128:(dto + 1) * 128].rearrange("(a p) c -> p a c", p=128), f"wp{pw}")
            dma("sync", wpb[pw][1].rearrange("p (a c) -> p a c", c=128), wprojs[1, :, dto * 128:(dto + 1) * 128].rearrange("(a p) c -> p a c", p=128), f"wp{pw}")
            bi = 4 + dto % 2
            wc("tensor", f"bk{bi}")
            wait("tensor", DS(f"wp{pw}"))
            first = True
            for dt in range(NDH):
                for (wb, ab_) in ((0, aoh), (0, aol), (1, aoh)):
                    last = (dt == NDH - 1 and wb == 1)
                    op("tensor", "matmul", PS[bi][0:128, 0:TT],
                       wpb[pw][wb][:, dt * 128:(dt + 1) * 128],
                       ab_[:, dt * TT:(dt + 1) * TT], start=first, stop=last, inc=sTE)
                    first = False
            sc_(f"wpb{pw}", sTE)
            wait("vector", sTE)
            op("vector", "scalar_tensor_tensor", xa[:, dto * TT:(dto + 1) * TT],
               PS[bi][0:128, 0:TT], bproj[:, dto:dto + 1],
               xa[:, dto * TT:(dto + 1) * TT], op0=ADD, op1=ADD)
            mile("vector", sVE)
            sc_(f"bk{bi}", sVE)
        sc_("ao", sVE)
        wait("scalar", sVE)
        wait("tensor", sVE)

        def y2_consume(dt, dt4, ts=ts):
            wait("vector", sTE)
            op("vector", "tensor_tensor", xa[:, dt * TT:(dt + 1) * TT],
               xa[:, dt * TT:(dt + 1) * TT], PS[dt4][0:128, 0:TT], op=ADD)
            mile("vector", sVE)
            sc_(f"bk{dt4}", sVE)
            wait("gpsimd", sVE)
            dma("gpsimd", out_d[:, ts].rearrange("(a p) t -> p a t", p=128)[:, dt, :],
                xa[:, dt * TT:(dt + 1) * TT], "st_x")

        if "wpb1" in cons:
            cons["w1b1"] = cons["wpb1"]
        ffn_tile(xa, bufs4, y2_consume)
        if "w1b1" in cons:
            cons["wpb1"] = cons["w1b1"]
        sc_("xa", DS("st_x"))
    for kk in list(_DS) if False else []:
        pass
    wait("gpsimd", DS("st_x"))

    with nc.Block() as block:
        @block.sync
        def _(e):
            for f in streams["sync"]:
                f(e)

        @block.vector
        def _(e):
            for f in streams["vector"]:
                f(e)

        @block.scalar
        def _(e):
            for f in streams["scalar"]:
                f(e)

        @block.tensor
        def _(e):
            for f in streams["tensor"]:
                f(e)

        @block.gpsimd
        def _(e):
            for f in streams["gpsimd"]:
                f(e)
    es.close()
    return nc


def _make_msel():
    m = np.zeros((128, 128), dtype=np.float32)
    for c in range(64):
        m[c, c] = 1.0
        if c < 63:
            m[64 + c, c] = 1.0
    m[127, 64] = 1.0
    return m


def make_inputs_for_core(core, full):
    b = core // 2
    half = core % 2
    x = np.asarray(full["L0_attn_x"], dtype=np.float32)
    TF = x.shape[1]
    xb = x[b]
    if half == 1:
        xb = np.concatenate([xb[TF // 2:], xb[:TF // 2]], axis=0)
    E = 8
    sel8 = np.zeros((E, E * 128), dtype=np.float32)
    for e in range(E):
        sel8[e, e * 128:(e + 1) * 128] = 1.0
    return {
        "x_t": np.ascontiguousarray(xb.T),
        "w_qkv": np.ascontiguousarray(np.asarray(full["L0_attn_w_qkv"], dtype=np.float32)),
        "w_proj": np.ascontiguousarray(np.asarray(full["L0_attn_w_proj"], dtype=np.float32)),
        "b_proj": np.ascontiguousarray(np.asarray(full["L0_attn_b_proj"], dtype=np.float32).reshape(-1, 128).T),
        "wg_t": np.ascontiguousarray(np.asarray(full["L0_ffn_Wg"], dtype=np.float32).T),
        "ln_scale_t": np.ascontiguousarray(np.asarray(full["L0_ffn_ln_scale"], dtype=np.float32).T),
        "ln_bias_t": np.ascontiguousarray(np.asarray(full["L0_ffn_ln_bias"], dtype=np.float32).T),
        "w1": np.ascontiguousarray(np.asarray(full["L0_ffn_W1"], dtype=np.float32)),
        "b1": np.ascontiguousarray(np.asarray(full["L0_ffn_b1"], dtype=np.float32)),
        "w2": np.ascontiguousarray(np.asarray(full["L0_ffn_W2"], dtype=np.float32)),
        "b2": np.ascontiguousarray(np.asarray(full["L0_ffn_b2"], dtype=np.float32)),
        "eye_d": np.eye(128, dtype=np.float32),
        "msel_d": _make_msel(),
        "sel8_d": sel8,
    }


def assemble_output(results, B, TF, D):
    out = np.empty((B, TF, D), dtype=np.float32)
    for core, r in enumerate(results):
        b = core // 2
        half = core % 2
        o = np.asarray(r["out_t"]).T
        out[b, half * (TF // 2):(half + 1) * (TF // 2)] = o
    return out


_CFG = {"D": 1024, "TF": 8192, "TO": 4096, "H": 16}


def kernel(**inputs):
    from concourse.bass_utils import run_bass_kernel_spmd
    nc = build(_CFG)
    in_maps = [make_inputs_for_core(c, inputs) for c in range(8)]
    res = run_bass_kernel_spmd(nc, in_maps, core_ids=list(range(8)))
    return assemble_output(res.results, 4, _CFG["TF"], _CFG["D"])
